# version 12
# baseline (speedup 1.0000x reference)
import numpy as np
import ml_dtypes
from contextlib import ExitStack
import concourse.bass as bass
import concourse.mybir as mybir
from concourse.bass_utils import run_bass_kernel_spmd

F32 = mybir.dt.float32
BF16 = mybir.dt.bfloat16
U8 = mybir.dt.uint8
U16 = mybir.dt.uint16
AF = mybir.ActivationFunctionType
ALU = mybir.AluOpType
AX = mybir.AxisListType
bfdt = ml_dtypes.bfloat16

NCORES = 8
M = 1024
MT = 8
DM = 4096
KT = 32
NT = 86
IDIM = 11008
OG = 8
IT = 86
L = 4096
INV_DM = 1.0 / 4096.0
EPS = 1e-5


def _host_consts():
    ar = np.arange(L, dtype=np.float32) / np.float32(L)
    Omega = np.exp((-2j * np.pi * ar).astype(np.complex64)).astype(np.complex64)
    w = ((np.complex64(1.0) - Omega) / (np.complex64(1.0) + Omega)).astype(np.complex64)
    c = (np.complex64(2.0) / (np.complex64(1.0) + Omega)).astype(np.complex64)
    cre = c.real.reshape(64, 64).astype(np.float32)
    cim = c.imag.reshape(64, 64).astype(np.float32)
    F64 = np.exp(2j * np.pi * np.arange(64)[:, None] * np.arange(64)[None, :] / 64.0) / 64.0
    tw = np.exp(2j * np.pi * np.arange(64)[:, None] * np.arange(64)[None, :] / 4096.0)
    c8 = np.stack([
        cre, cim,
        F64.real.astype(np.float32), F64.imag.astype(np.float32),
        (-F64.imag).astype(np.float32),
        tw.real.astype(np.float32), tw.imag.astype(np.float32),
        np.eye(64, dtype=np.float32),
    ]).astype(np.float32)
    oh32 = np.zeros((128, 32 * 32), dtype=np.float32)
    for I in range(32):
        oh32[:, 32 * I + I] = 1.0
    oh86 = np.zeros((128, 86 * 86), dtype=np.float32)
    for I in range(86):
        oh86[:, 86 * I + I] = 1.0
    return {
        "wre": np.ascontiguousarray(w.real.reshape(1, L).astype(np.float32)),
        "wim": np.ascontiguousarray(w.imag.reshape(1, L).astype(np.float32)),
        "c6464": c8,
        "ones164": np.ones((1, 64), dtype=np.float32),
        "id128": np.eye(128, dtype=np.float32).astype(bfdt),
        "oh32": oh32.astype(bfdt),
        "oh86": oh86.astype(bfdt),
    }


def _tile_qweight_gu(qw):
    q8 = qw.view(np.uint8)
    t = q8.reshape(KT, 128, NT, 64).transpose(2, 1, 0, 3)
    return np.ascontiguousarray(t.reshape(NT, 128, 2048))


def _tile_qweight_down(qw):
    q8 = qw.view(np.uint8)
    t = q8.reshape(IT, 128, OG, 256).transpose(2, 0, 1, 3)
    return np.ascontiguousarray(t)


def _srep_gu(sc):
    s4 = sc.reshape(KT, NT, 64, 2)
    t = s4.transpose(1, 3, 0, 2).reshape(NT, 2, 2048)
    out = np.broadcast_to(t[:, :, None, :], (NT, 2, 128, 2048))
    return np.ascontiguousarray(out.astype(np.float32))


def _srep_down(sc):
    s4 = sc.reshape(IT, OG, 256, 2)
    t = s4.transpose(1, 0, 3, 2)
    out = np.broadcast_to(t[:, :, :, None, :], (OG, IT, 2, 128, 256))
    return np.ascontiguousarray(out.astype(np.float32))


def _par_view2(ap2d, par, period=2):
    n = ap2d.shape[1]
    r = ap2d.rearrange("p (c t) -> p c t", t=period)
    return r[:, :, par:par + 1].squeeze(2)


def build_nc():
    nc = bass.Bass("TRN2", target_bir_lowering=False)
    dt = nc.dram_tensor
    x_d = dt("x", [M, DM], F32, kind="ExternalInput").ap()
    lam_d = dt("lam", [64, 2], F32, kind="ExternalInput").ap()
    pv_d = dt("pv", [64, 2], F32, kind="ExternalInput").ap()
    qv_d = dt("qv", [64, 2], F32, kind="ExternalInput").ap()
    vc_d = dt("vc", [64, 128], F32, kind="ExternalInput").ap()
    ct_d = dt("ct", [64, 2], F32, kind="ExternalInput").ap()
    bp_d = dt("bp", [64, 1], F32, kind="ExternalInput").ap()
    dd_d = dt("dd", [1, 1], F32, kind="ExternalInput").ap()
    lsr_d = dt("lsr", [64, 1], F32, kind="ExternalInput").ap()
    wre_d = dt("wre", [1, L], F32, kind="ExternalInput").ap()
    wim_d = dt("wim", [1, L], F32, kind="ExternalInput").ap()
    c64_d = dt("c6464", [8, 64, 64], F32, kind="ExternalInput").ap()
    ones164_d = dt("ones164", [1, 64], F32, kind="ExternalInput").ap()
    id128_d = dt("id128", [128, 128], BF16, kind="ExternalInput").ap()
    oh32_d = dt("oh32", [128, 1024], BF16, kind="ExternalInput").ap()
    oh86_d = dt("oh86", [128, 7396], BF16, kind="ExternalInput").ap()
    qg_d = dt("qg", [NT, 128, 2048], U8, kind="ExternalInput").ap()
    qu_d = dt("qu", [NT, 128, 2048], U8, kind="ExternalInput").ap()
    qd_d = dt("qd", [OG, IT, 128, 256], U8, kind="ExternalInput").ap()
    sg_rep_d = dt("sg_rep", [NT, 2, 128, 2048], F32, kind="ExternalInput").ap()
    su_rep_d = dt("su_rep", [NT, 2, 128, 2048], F32, kind="ExternalInput").ap()
    sd_rep_d = dt("sd_rep", [OG, IT, 2, 128, 256], F32, kind="ExternalInput").ap()
    sg_nat_d = dt("sg_nat", [32, IDIM], F32, kind="ExternalInput").ap()
    su_nat_d = dt("su_nat", [32, IDIM], F32, kind="ExternalInput").ap()
    sd_nat_d = dt("sd_nat", [IT, DM], F32, kind="ExternalInput").ap()
    zg_d = dt("zg", [32, 5504], U8, kind="ExternalInput").ap()
    zu_d = dt("zu", [32, 5504], U8, kind="ExternalInput").ap()
    zd_d = dt("zd", [IT, 2048], U8, kind="ExternalInput").ap()
    out_d = dt("out", [M, DM], F32, kind="ExternalOutput").ap()
    x1_d = dt("x1buf", [M, DM], F32, kind="Internal").ap()
    h_d = dt("hbuf", [IT, 128, M], BF16, kind="Internal").ap()
    kb_d = dt("kbuf", [1, 8320], BF16, kind="Internal").ap()
    kv_d = dt("kvbuf", [8, 4096], F32, kind="Internal").ap()

    es_glob = ExitStack()

    def sb(es, name, shape, dtp, **kw):
        return es.enter_context(nc.sbuf_tensor(name, shape, dtp, **kw))

    id128_s = sb(es_glob, "id128s", [128, 128], BF16)
    lam_s = sb(es_glob, "lams", [64, 2], F32)
    pv_s = sb(es_glob, "pvs", [64, 2], F32)
    qv_s = sb(es_glob, "qvs", [64, 2], F32)
    vc_s = sb(es_glob, "vcs", [64, 128], F32)
    ct_s = sb(es_glob, "cts", [64, 2], F32)
    bp_s = sb(es_glob, "bps", [64, 1], F32)
    dd_s = sb(es_glob, "dds", [1, 1], F32)
    lsr_s = sb(es_glob, "lsrs", [64, 1], F32)
    c64_s = sb(es_glob, "c64s", [64, 512], F32)
    ones164_s = sb(es_glob, "ones164s", [1, 64], F32)
    ts2_s = sb(es_glob, "ts2s", [64, 2], F32)
    oh32_s = sb(es_glob, "oh32s", [128, 1024], BF16)
    sum_s = sb(es_glob, "sums", [128, 2], F32)
    ssq_s = sb(es_glob, "ssqs", [128, 2], F32)
    mean_s = sb(es_glob, "means", [128, 2], F32)
    msq_s = sb(es_glob, "msqs", [128, 2], F32)
    var_s = sb(es_glob, "vars", [128, 2], F32)
    sdc_s = sb(es_glob, "sdcols", [128, 2], F32)
    rstd_s = sb(es_glob, "rstds", [128, 2], F32)
    gsf_s = sb(es_glob, "gsf", [32, M], F32)
    ngs_hi_s = sb(es_glob, "ngshi", [64, M], BF16)
    ngs_lo_s = sb(es_glob, "ngslo", [64, M], BF16)
    vdj_s = sb(es_glob, "vdjunk", [128, 192], F32)
    eps_s = sb(es_glob, "epscol", [128, 1], F32)
    sdj_s = sb(es_glob, "sdjunk", [128, 64], F32)
    zs_gu_s = sb(es_glob, "zsgu", [64, IDIM], BF16)
    zs_d_s = sb(es_glob, "zsd", [128, DM], BF16)

    es_xn2 = ExitStack()
    xn2T_s = sb(es_xn2, "xn2T", [128, KT * M], BF16, side="right")

    def mk(name):
        return es_glob.enter_context(nc.semaphore(name))
    s_small = mk("ssmall"); s_x = mk("sx"); s_x1w = mk("sx1w")
    s_k = mk("sk"); s_strip = mk("sstrip")
    s_asq = mk("sasq"); s_ava = mk("sava"); s_asd = mk("sasd")
    s_axn = mk("saxn"); s_atp = mk("satp"); s_atpe = mk("satpe")
    s_acv = mk("sacv"); s_ace = mk("sace"); s_ags = mk("sags")
    s_agse = mk("sagse"); s_zs = mk("szs"); s_zdma = mk("szdma"); s_zsb = mk("szsb")
    s_pk = mk("spk"); s_sr = mk("ssr"); s_wrdy = mk("swrdy")
    s_G = mk("sG"); s_U = mk("sU"); s_sl = mk("ssl")
    s_hm = mk("shm"); s_hw = mk("shw")
    s_oh = mk("soh"); s_ghr = mk("sghr"); s_gti = mk("sgti"); s_gshe = mk("sgshe")
    s_pkd = mk("spkd"); s_sd = mk("ssd"); s_wd = mk("swd")
    s_cti = mk("scti"); s_cmm = mk("scmm"); s_x1r = mk("sx1r")
    s_cev = mk("scev"); s_out = mk("sout"); s_hr2 = mk("shr2")
    sx_sl = [mk("sxa"), mk("sxb")]
    sx1w_sl = [mk("sx1wa"), mk("sx1wb")]
    spk_sl = [mk("spka"), mk("spkb")]
    ssr_sl = [mk("ssra"), mk("ssrb")]
    shw_sl = [mk("shwa"), mk("shwb"), mk("shwc")]
    sghr_sl = [mk(f"sghr{j}") for j in range(4)]
    spkd_sl = [mk("spkda"), mk("spkdb")]
    ssd_sl = [mk("ssda"), mk("ssdb")]
    sx1r_sl = [mk(f"sx1r{j}") for j in range(8)]
    sout_sl = [mk(f"sout{j}") for j in range(8)]
    s_zq = mk("szq"); s_ngsd = mk("sngsd")
    s_kd = mk("skd"); s_kg = mk("skg")

    def vdum(v):
        return v.tensor_copy(vdj_s[:], vdj_s[:])

    def sdum(s):
        return s.activation(sdj_s[:], oh32_s[:, 0:64], AF.Identity, bias=0.0, scale=1.0)

    es_A = ExitStack()
    xsl = [sb(es_A, f"xsl{j}", [128, DM], F32) for j in range(2)]
    xnsl = [sb(es_A, f"xnsl{j}", [128, DM], BF16) for j in range(2)]
    xn1T = [sb(es_A, f"xn1T{j}", [128, DM], BF16) for j in range(2)]
    strip_s = sb(es_A, "strip", [128, 8192], BF16)
    wtab = [sb(es_A, f"wtab{j}", [1, 256], F32) for j in range(2)]
    den_re_s = sb(es_A, "denre", [64, 256], F32)
    den_im_s = sb(es_A, "denim", [64, 256], F32)
    mag_s = sb(es_A, "mags", [64, 256], F32)
    mag2_s = sb(es_A, "mags2", [64, 256], F32)
    ef_s = sb(es_A, "efs", [128, 256], F32)
    kvch_s = sb(es_A, "kvch", [8, 256], F32)
    atile_s = sb(es_A, "atile", [64, 8 * 64], F32)
    scr_s = sb(es_A, "scr64", [64, 12 * 64], F32)
    vcreT_s = sb(es_A, "vcret", [64, 64], F32)
    vcimT_s = sb(es_A, "vcimt", [64, 64], F32)
    bre_s = sb(es_A, "bres", [64, 1], F32)
    bim_s = sb(es_A, "bims", [64, 1], F32)
    vmat_s = sb(es_A, "vmats", [128, 8], F32)
    krows_s = sb(es_A, "krows", [64, 64], BF16)
    zerot_s = sb(es_A, "zerot", [128, 33], BF16)

    es_psA = ExitStack()
    ps_tp = [es_psA.enter_context(nc.psum_tensor(f"pstp{j}", [128, 512], F32)) for j in range(2)]
    ps_tpb = [t[:].bitcast(BF16) for t in ps_tp]
    ps_gs = es_psA.enter_context(nc.psum_tensor("psgs", [32, M], F32))
    ps_quad = es_psA.enter_context(nc.psum_tensor("psquad", [128, 2048], F32))
    ps_wre = ps_quad[0:64, 0:256]
    ps_wim = ps_quad[0:64, 512:768]
    ps_kv = ps_quad[0:8, 1024:1280]
    ps_c1 = ps_quad[0:64, 1536:1600]
    ps_c2 = ps_quad[0:64, 1600:1664]
    ps_vt = ps_quad[0:64, 1664:1728]
    ps_b = ps_quad[0:64, 1728:1729]

    ident64 = c64_s[:, 7 * 64:8 * 64]
    lam_re = lam_s[:, 0:1]; lam_im = lam_s[:, 1:2]
    ts2col = ts2_s[:, 1:2]

    def scr(i):
        return scr_s[:, 64 * i:64 * (i + 1)]

    def at(i):
        return atile_s[:, 64 * i:64 * (i + 1)]

    ksched = []

    def kstep(eng_key, inc, fn):
        ksched.append((eng_key, inc, fn))

    kstep("vector", 1, lambda v: v.memset(zerot_s[:], 0.0))
    kstep("scalar", 1, lambda s: s.activation(ts2_s[:, 0:1], lsr_s[:], AF.Exp))

    def _v_ts2(v):
        v.reciprocal(ts2_s[:, 1:2], ts2_s[:, 0:1])
        vdum(v)
        return v.tensor_scalar_mul(ts2col, ts2col, 2.0)
    kstep("vector", 1, _v_ts2)
    def _t_tpre(t):
        t.transpose(ps_vt, vc_s[:, 0:128:2], ident64)
        return t.matmul(ps_vt[:, 63:64], vc_s[:, 0:128:2], ident64[:, 63:64],
                        start=True, stop=True)
    kstep("tensor", 1, _t_tpre)
    kstep("vector", 1, lambda v: v.tensor_copy(vcreT_s[:], ps_vt))
    def _t_tpim(t):
        t.transpose(ps_vt, vc_s[:, 1:128:2], ident64)
        return t.matmul(ps_vt[:, 63:64], vc_s[:, 1:128:2], ident64[:, 63:64],
                        start=True, stop=True)
    kstep("tensor", 1, _t_tpim)
    kstep("vector", 1, lambda v: v.tensor_copy(vcimT_s[:], ps_vt))
    kstep("tensor", 1, lambda t: t.matmul(ps_b, vcreT_s[:], bp_s[:], start=True, stop=True))
    kstep("vector", 1, lambda v: v.tensor_copy(bre_s[:], ps_b))
    kstep("tensor", 1, lambda t: t.matmul(ps_b, vcimT_s[:], bp_s[:], start=True, stop=True))
    kstep("vector", 1, lambda v: v.tensor_copy(bim_s[:], ps_b))

    def _v_quartet(v):
        pairs = [
            (ct_s[:, 0:1], ct_s[:, 1:2], bre_s[:], bim_s[:]),
            (ct_s[:, 0:1], ct_s[:, 1:2], pv_s[:, 0:1], pv_s[:, 1:2]),
            (qv_s[:, 0:1], qv_s[:, 1:2], bre_s[:], bim_s[:]),
            (qv_s[:, 0:1], qv_s[:, 1:2], pv_s[:, 0:1], pv_s[:, 1:2]),
        ]
        for idx, (are, aim, brr, bii) in enumerate(pairs):
            c = scr_s[:, 4 * idx:4 * idx + 4]
            v.tensor_mul(c[:, 0:1], are, brr)
            v.tensor_mul(c[:, 1:2], aim, bii)
            v.tensor_mul(c[:, 2:3], are, bii)
            v.tensor_mul(c[:, 3:4], aim, brr)
        vdum(v)
        for idx in range(4):
            c = scr_s[:, 4 * idx:4 * idx + 4]
            v.tensor_add(scr_s[:, 16 + 2 * idx:17 + 2 * idx], c[:, 0:1], c[:, 1:2])
            v.tensor_sub(scr_s[:, 17 + 2 * idx:18 + 2 * idx], c[:, 2:3], c[:, 3:4])
        vdum(v)
        last = None
        for idx in range(4):
            vre = scr_s[:, 16 + 2 * idx:17 + 2 * idx]
            vim = scr_s[:, 17 + 2 * idx:18 + 2 * idx]
            cre_c, cim_c = 2 * idx, 2 * idx + 1
            v.tensor_copy(vmat_s[0:64, cre_c:cre_c + 1], vre)
            v.tensor_copy(vmat_s[0:64, cim_c:cim_c + 1], vim)
            v.tensor_copy(vmat_s[64:128, cre_c:cre_c + 1], vim)
            last = v.tensor_scalar_mul(vmat_s[64:128, cim_c:cim_c + 1], vre, -1.0)
        return last
    kstep("vector", 1, _v_quartet)

    for c8 in range(16):
        sl = 0
        kstep("sync", 16, lambda sy, c8=c8, sl=sl: sy.dma_start(
            out=wtab[sl][:], in_=wre_d[0:1, 256 * c8:256 * (c8 + 1)]))
        kstep("sync", 16, lambda sy, c8=c8, sl=sl: sy.dma_start(
            out=wtab[sl + 1][:], in_=wim_d[0:1, 256 * c8:256 * (c8 + 1)]))

        def _t_bc(t, sl=sl):
            t.matmul(ps_wre, ones164_s[:], wtab[sl][:], start=True, stop=True)
            return t.matmul(ps_wim, ones164_s[:], wtab[sl + 1][:], start=True, stop=True)
        kstep("tensor", 1, _t_bc)

        def _v_den(v):
            v.tensor_scalar(den_re_s[:], ps_wre, ts2col, lam_re, ALU.mult, ALU.subtract)
            v.tensor_scalar(den_im_s[:], ps_wim, ts2col, lam_im, ALU.mult, ALU.subtract)
            v.tensor_mul(mag_s[:], den_re_s[:], den_re_s[:])
            v.tensor_mul(mag2_s[:], den_im_s[:], den_im_s[:])
            v.tensor_add(mag_s[:], mag_s[:], mag2_s[:])
            v.reciprocal(mag_s[:], mag_s[:])
            v.tensor_mul(ef_s[0:64, :], den_re_s[:], mag_s[:])
            return v.tensor_mul(ef_s[64:128, :], den_im_s[:], mag_s[:])
        kstep("vector", 1, _v_den)
        kstep("tensor", 1, lambda t: t.matmul(ps_kv, vmat_s[:], ef_s[:], start=True, stop=True))
        kstep("vector", 1, lambda v: v.tensor_copy(kvch_s[:], ps_kv))
        kstep("gpsimd", 16, lambda g, c8=c8: g.dma_start(
            out=kv_d[:, 256 * c8:256 * (c8 + 1)], in_=kvch_s[:]))

    def _g_kvback(g):
        for v in range(8):
            g.dma_start(
                out=at(v),
                in_=kv_d[v:v + 1, :].rearrange("a (p f) -> (a p) f", f=64)
            ).then_inc(s_kg, 16)
        return None
    kstep("gpsimd", 16 * 8, _g_kvback)

    def _v_rational(v):
        tre = scr(0); tim = at(7)
        v.tensor_scalar(tre, at(6), 1.0, None, ALU.add)
        mg = scr(1); m2 = scr(2)
        v.tensor_mul(mg, tre, tre)
        v.tensor_mul(m2, tim, tim)
        v.tensor_add(mg, mg, m2)
        v.reciprocal(mg, mg)
        nre = scr(3); nim = scr(4)
        v.tensor_mul(m2, at(2), at(4))
        v.tensor_mul(nre, at(3), at(5))
        v.tensor_sub(nre, m2, nre)
        v.tensor_mul(m2, at(2), at(5))
        v.tensor_mul(nim, at(3), at(4))
        v.tensor_add(nim, m2, nim)
        wr = scr(5); wi = scr(6)
        v.tensor_mul(m2, nre, tre)
        v.tensor_mul(wr, nim, tim)
        v.tensor_add(wr, m2, wr)
        v.tensor_mul(wr, wr, mg)
        v.tensor_mul(m2, nim, tre)
        v.tensor_mul(wi, nre, tim)
        v.tensor_sub(wi, m2, wi)
        v.tensor_mul(wi, wi, mg)
        are = scr(7); aim = scr(8)
        v.tensor_sub(are, at(0), wr)
        v.tensor_sub(aim, at(1), wi)
        cre_t = c64_s[:, 0:64]; cim_t = c64_s[:, 64:128]
        atre = scr(9); atim = scr(10)
        v.tensor_mul(m2, are, cre_t)
        v.tensor_mul(atre, aim, cim_t)
        v.tensor_sub(atre, m2, atre)
        v.tensor_mul(m2, are, cim_t)
        v.tensor_mul(atim, aim, cre_t)
        return v.tensor_add(atim, m2, atim)
    kstep("vector", 1, _v_rational)

    f64re = c64_s[:, 2 * 64:3 * 64]
    f64im = c64_s[:, 3 * 64:4 * 64]
    f64imn = c64_s[:, 4 * 64:5 * 64]
    twre = c64_s[:, 5 * 64:6 * 64]
    twim = c64_s[:, 6 * 64:7 * 64]

    def _t_sstep(t):
        t.matmul(ps_c1, scr(9), f64re, start=True, stop=False)
        t.matmul(ps_c1, scr(10), f64imn, start=False, stop=True)
        t.matmul(ps_c2, scr(9), f64im, start=True, stop=False)
        return t.matmul(ps_c2, scr(10), f64re, start=False, stop=True)
    kstep("tensor", 1, _t_sstep)

    def _v_twiddle(v):
        sre = scr(0); sim = scr(1)
        v.tensor_copy(sre, ps_c1)
        v.tensor_copy(sim, ps_c2)
        m2 = scr(2); tr = scr(3); ti = scr(4)
        v.tensor_mul(m2, sre, twre)
        v.tensor_mul(tr, sim, twim)
        v.tensor_sub(tr, m2, tr)
        v.tensor_mul(m2, sre, twim)
        v.tensor_mul(ti, sim, twre)
        return v.tensor_add(ti, m2, ti)
    kstep("vector", 1, _v_twiddle)

    def _t_xstep(t):
        t.matmul(ps_c1, scr(3), f64re, start=True, stop=False)
        return t.matmul(ps_c1, scr(4), f64imn, start=False, stop=True)
    kstep("tensor", 1, _t_xstep)

    def _v_xev(v):
        v.tensor_copy(scr(5), ps_c1)
        return v.tensor_tensor(scr(5)[0:1, 0:1], scr(5)[0:1, 0:1], dd_s[:], ALU.add)
    kstep("vector", 1, _v_xev)
    def _t_tpk(t):
        t.transpose(ps_c2, scr(5), ident64)
        return t.matmul(ps_c2[:, 63:64], scr(5), ident64[:, 63:64],
                        start=True, stop=True)
    kstep("tensor", 1, _t_tpk)
    kstep("vector", 1, lambda v: v.tensor_copy(krows_s[:], ps_c2))

    def _g_kbuf(g):
        g.dma_start(out=kb_d[0:1, 0:4224], in_=zerot_s[:]).then_inc(s_kg, 16)
        g.dma_start(out=kb_d[0:1, 4224:8320], in_=krows_s[:]).then_inc(s_kg, 16)
        return None
    kstep("gpsimd", 32, _g_kbuf)

    def _sc_strip(s):
        for p in range(128):
            s.dma_start(out=strip_s[p:p + 1, :],
                        in_=kb_d[0:1, 128 - p:128 - p + 8192]).then_inc(s_strip, 16)
        return None
    kstep("scalar", 0, _sc_strip)

    def run_ksched(key, eng):
        te = 0
        td = 0
        tg = 0
        for ek, inc, fn in ksched:
            if ek == key:
                if te > 0:
                    eng.wait_ge(s_k, te)
                if td > 0:
                    eng.wait_ge(s_kd, td)
                if tg > 0:
                    eng.wait_ge(s_kg, tg)
                if ek == "vector" and (te + td + tg) > 0:
                    vdum(eng)
                last = fn(eng)
                if inc > 0 and last is not None:
                    if ek == "vector":
                        vdum(eng).then_inc(s_k, inc)
                    elif ek == "scalar":
                        sdum(eng).then_inc(s_k, inc)
                    elif ek == "sync":
                        last.then_inc(s_kd, inc)
                    elif ek == "gpsimd":
                        last.then_inc(s_kg, inc)
                    else:
                        last.then_inc(s_k, inc)
            if ek == "sync":
                td += inc
            elif ek == "gpsimd":
                tg += inc
            else:
                te += inc

    def conv_ops(half):
        ops = []
        for I in range(32):
            for g4 in range(4):
                nbase = half * 2048 + g4 * 512
                Imax = min((nbase + 511) // 128, 31)
                if I <= Imax:
                    ops.append((I, g4, nbase, I == 0, I == Imax))
        return ops
    CONV = [conv_ops(0), conv_ops(1)]

    small_list = [
        (lam_s, lam_d), (pv_s, pv_d), (qv_s, qv_d), (vc_s, vc_d),
        (ct_s, ct_d), (bp_s, bp_d), (dd_s, dd_d), (lsr_s, lsr_d),
        (id128_s, id128_d), (ones164_s, ones164_d), (oh32_s, oh32_d),
    ]
    SMALL_TOT = 16 * (len(small_list) + 1)

    xn2T3 = xn2T_s[:].rearrange("p (I m) -> p I m", m=M)

    with nc.Block() as block:

        @block.sync
        def _(sync):
            for dst, src in small_list:
                sync.dma_start(out=dst[:], in_=src[:]).then_inc(s_small, 16)
            sync.dma_start(
                out=c64_s[:].rearrange("p (t f) -> p t f", f=64),
                in_=c64_d[:].transpose([1, 0, 2]),
            ).then_inc(s_small, 16)
            run_ksched("sync", sync)

        @block.gpsimd
        def _(gp):
            for i in range(2):
                gp.dma_start(out=xsl[i % 2][:], in_=x_d[128 * i:128 * (i + 1), :]).then_inc(sx_sl[i % 2], 16)
            gp.wait_ge(s_small, SMALL_TOT)
            run_ksched("gpsimd", gp)
            for i in range(2, MT):
                gp.wait_ge(sx1w_sl[i % 2], 16 * ((i - 2) // 2 + 1))
                gp.wait_ge(s_axn, 2 * i - 2)
                gp.dma_start(out=xsl[i % 2][:], in_=x_d[128 * i:128 * (i + 1), :]).then_inc(sx_sl[i % 2], 16)

        @block.scalar
        def _(scalar):
            scalar.wait_ge(s_small, SMALL_TOT)
            run_ksched("scalar", scalar)
            for i in range(MT):
                scalar.wait_ge(s_ava, 2 * i + 1)
                scalar.activation(sdc_s[:, 0:1], var_s[:, 0:1], AF.Sqrt, bias=eps_s[:])
                sdum(scalar).then_inc(s_asd, 1)
                scalar.wait_ge(s_ava, 2 * i + 2)
                scalar.activation(sdc_s[:, 1:2], var_s[:, 1:2], AF.Sqrt, bias=eps_s[:])
                sdum(scalar).then_inc(s_asd, 1)
                scalar.dma_start(out=x1_d[128 * i:128 * (i + 1), :],
                                 in_=xsl[i % 2][:]).then_inc(sx1w_sl[i % 2], 16)

        @block.tensor
        def _(tensor):
            tensor.wait_ge(s_small, SMALL_TOT)
            run_ksched("tensor", tensor)
            for i in range(MT):
                tensor.wait_ge(s_axn, 2 * i + 1)
                for g in range(8):
                    gidx = 16 * i + g
                    if gidx >= 2:
                        tensor.wait_ge(s_atpe, gidx - 1)
                    for j in range(4):
                        col = 4 * g + j
                        tensor.transpose(
                            ps_tpb[g % 2][:, 128 * j:128 * (j + 1)],
                            xnsl[i % 2][:, 128 * col:128 * (col + 1)], id128_s[:])
                    tensor.matmul(
                        ps_tp[g % 2][:, 511:512],
                        xnsl[i % 2][:, 128 * (4 * g + 3):128 * (4 * g + 4)],
                        id128_s[:, 127:128], start=True, stop=True).then_inc(s_atp, 1)
                if i == 0:
                    tensor.wait_ge(s_strip, 16 * 128)
                tensor.wait_ge(s_atpe, 16 * i + 8)
                if i > 0:
                    tensor.wait_ge(s_ace, 2 * i)
                for half in range(2):
                    if half == 1:
                        tensor.wait_ge(s_ace, 2 * i + 1)
                    for (I, g4, nbase, first, lastf) in CONV[half]:
                        f0 = nbase - 128 * I + 4096
                        inst = tensor.matmul(
                            ps_quad[:, 512 * g4:512 * (g4 + 1)],
                            xn1T[i % 2][:, 128 * I:128 * (I + 1)],
                            strip_s[:, f0:f0 + 512],
                            start=first, stop=lastf)
                    inst.then_inc(s_acv, 1)
                tensor.wait_ge(s_axn, 2 * i + 2)
                for g in range(8):
                    gidx = 16 * i + 8 + g
                    tensor.wait_ge(s_atpe, gidx - 1)
                    for j in range(4):
                        col = 4 * g + j
                        tensor.transpose(
                            ps_tpb[g % 2][:, 128 * j:128 * (j + 1)],
                            xnsl[i % 2][:, 128 * col:128 * (col + 1)], id128_s[:])
                    tensor.matmul(
                        ps_tp[g % 2][:, 511:512],
                        xnsl[i % 2][:, 128 * (4 * g + 3):128 * (4 * g + 4)],
                        id128_s[:, 127:128], start=True, stop=True).then_inc(s_atp, 1)
                tensor.wait_ge(s_atpe, 16 * (i + 1))
                for I in range(KT):
                    inst = tensor.matmul(
                        ps_gs[:, 128 * i:128 * (i + 1)],
                        oh32_s[:, 32 * I:32 * (I + 1)],
                        xn2T3[:, I, 128 * i:128 * (i + 1)],
                        start=(I == 0), stop=(I == KT - 1))
                if i == MT - 1:
                    inst.then_inc(s_ags, 1)

        @block.vector
        def _(vector):
            vector.memset(eps_s[:], EPS)
            vector.memset(vdj_s[:], 0.0)
            vector.wait_ge(s_small, SMALL_TOT)
            run_ksched("vector", vector)
            for i in range(MT):
                b = i % 2
                vector.wait_ge(sx_sl[i % 2], 16 * (i // 2 + 1))
                if i >= 2:
                    vector.wait_ge(s_atp, 16 * (i - 1))
                vector.reduce_sum(sum_s[:, 0:1], xsl[b][:], AX.X)
                vector.tensor_mul(xnsl[b][:], xsl[b][:], xsl[b][:])
                vector.reduce_sum(ssq_s[:, 0:1], xnsl[b][:], AX.X)
                vector.tensor_scalar(mean_s[:, 0:1], sum_s[:, 0:1], INV_DM, None, ALU.mult)
                vdum(vector)
                vector.tensor_mul(var_s[:, 0:1], mean_s[:, 0:1], mean_s[:, 0:1])
                vdum(vector)
                vector.scalar_tensor_tensor(
                    var_s[:, 0:1], ssq_s[:, 0:1], INV_DM, var_s[:, 0:1],
                    ALU.mult, ALU.subtract)
                vdum(vector).then_inc(s_ava, 1)
                vector.wait_ge(s_asd, 2 * i + 1)
                vdum(vector)
                vector.reciprocal(rstd_s[:, 0:1], sdc_s[:, 0:1])
                vdum(vector)
                vector.tensor_scalar(
                    xnsl[b][:], xsl[b][:], mean_s[:, 0:1], rstd_s[:, 0:1],
                    ALU.subtract, ALU.mult)
                vdum(vector).then_inc(s_axn, 1)
                for g in range(8):
                    gidx = 16 * i + g
                    vector.wait_ge(s_atp, gidx + 1)
                    vdum(vector)
                    vector.tensor_copy(
                        xn1T[b][:, 512 * g:512 * (g + 1)], ps_tpb[g % 2][:, 0:512])
                    vdum(vector).then_inc(s_atpe, 1)
                for half in range(2):
                    vector.wait_ge(s_acv, 2 * i + half + 1)
                    vdum(vector)
                    vector.scalar_tensor_tensor(
                        xsl[b][:, 2048 * half:2048 * (half + 1)],
                        ps_quad[:, 0:2048], 1.0,
                        xsl[b][:, 2048 * half:2048 * (half + 1)],
                        ALU.mult, ALU.add).then_inc(s_ace, 1)
                vector.reduce_sum(sum_s[:, 1:2], xsl[b][:], AX.X)
                vector.tensor_mul(xnsl[b][:], xsl[b][:], xsl[b][:])
                vector.reduce_sum(ssq_s[:, 1:2], xnsl[b][:], AX.X)
                vector.tensor_scalar(mean_s[:, 1:2], sum_s[:, 1:2], INV_DM, None, ALU.mult)
                vdum(vector)
                vector.tensor_mul(var_s[:, 1:2], mean_s[:, 1:2], mean_s[:, 1:2])
                vdum(vector)
                vector.scalar_tensor_tensor(
                    var_s[:, 1:2], ssq_s[:, 1:2], INV_DM, var_s[:, 1:2],
                    ALU.mult, ALU.subtract)
                vdum(vector).then_inc(s_ava, 1)
                vector.wait_ge(s_asd, 2 * i + 2)
                vdum(vector)
                vector.reciprocal(rstd_s[:, 1:2], sdc_s[:, 1:2])
                vdum(vector)
                vector.tensor_scalar(
                    xnsl[b][:], xsl[b][:], mean_s[:, 1:2], rstd_s[:, 1:2],
                    ALU.subtract, ALU.mult)
                vdum(vector).then_inc(s_axn, 1)
                for g in range(8):
                    gidx = 16 * i + 8 + g
                    vector.wait_ge(s_atp, gidx + 1)
                    vdum(vector)
                    dst2 = xn2T3[:, 4 * g:4 * g + 4, 128 * i:128 * (i + 1)]
                    src2 = ps_tpb[g % 2][:, 0:512].rearrange("p (j r) -> p j r", r=128)
                    vector.tensor_copy(dst2, src2)
                    vdum(vector).then_inc(s_atpe, 1)
            vector.wait_ge(s_ags, 1)
            vdum(vector)
            vector.tensor_scalar(gsf_s[:], ps_gs[:], -1.0, None, ALU.mult)
            vector.tensor_copy(ngs_hi_s[0:32, :], gsf_s[:])
            vector.tensor_tensor(ngs_lo_s[0:32, :], gsf_s[:], ngs_hi_s[0:32, :],
                                 ALU.subtract)
            vdum(vector).then_inc(s_agse, 1)

    es_psA.close()
    es_A.close()

    es_zsb = ExitStack()
    zq_g2 = sb(es_zsb, "zqg2", [32, 5504], U8)
    zq_u2 = sb(es_zsb, "zqu2", [32, 5504], U8)
    zq_dn = sb(es_zsb, "zqdn", [86, 2048], U8)
    zlo = sb(es_zsb, "zlo", [32, 5504], U8)
    zhi = sb(es_zsb, "zhi", [32, 5504], U8)
    zdlo = sb(es_zsb, "zdlo", [86, 2048], U8)
    zdhi = sb(es_zsb, "zdhi", [86, 2048], U8)
    snat_gu = sb(es_zsb, "snatgu", [32, 2752], F32)
    snat_dn = sb(es_zsb, "snatd", [86, 1024], F32)

    BARRIER_A = [(s_ags, 1), (s_agse, 1), (s_asd, 16),
                 (sx1w_sl[0], 16 * 4), (sx1w_sl[1], 16 * 4),
                 (sx_sl[0], 16 * 4), (sx_sl[1], 16 * 4)]

    with nc.Block() as block:

        @block.sync
        def _(sync):
            for sem, vv in BARRIER_A:
                sync.wait_ge(sem, vv)
            sync.dma_start(out=ngs_hi_s[32:64, :], in_=ngs_hi_s[0:32, :]).then_inc(s_ngsd, 16)
            sync.dma_start(out=ngs_lo_s[32:64, :], in_=ngs_lo_s[0:32, :]).then_inc(s_ngsd, 16)
            sync.dma_start(out=zq_g2[:], in_=zg_d[:]).then_inc(s_zq, 16)
            sync.dma_start(out=zq_u2[:], in_=zu_d[:]).then_inc(s_zq, 16)
            sync.dma_start(out=zq_dn[:], in_=zd_d[:]).then_inc(s_zq, 16)
            for mi, srcd in enumerate([sg_nat_d, su_nat_d]):
                for q4 in range(4):
                    ci = mi * 4 + q4
                    if ci > 0:
                        sync.wait_ge(s_zsb, ci)
                    sync.dma_start(out=snat_gu[:],
                                   in_=srcd[:, 2752 * q4:2752 * (q4 + 1)]).then_inc(s_zdma, 16)
            for q4 in range(4):
                ci = 8 + q4
                sync.wait_ge(s_zsb, ci)
                sync.dma_start(out=snat_dn[:],
                               in_=sd_nat_d[:, 1024 * q4:1024 * (q4 + 1)]).then_inc(s_zdma, 16)

        @block.vector
        def _(vector):
            for sem, vv in BARRIER_A:
                vector.wait_ge(sem, vv)
            vector.memset(zs_d_s[:], 0.0)
            vector.wait_ge(s_zq, 48)
            ci = 0
            for mi, zq_m in enumerate([zq_g2, zq_u2]):
                rows = slice(32 * mi, 32 * mi + 32)
                zq16 = zq_m[:].bitcast(U16)
                vector.tensor_single_scalar(zlo[:].bitcast(U16), zq16, 0x0F0F, ALU.bitwise_and)
                vector.tensor_single_scalar(zhi[:].bitcast(U16), zq16, 4, ALU.logical_shift_right)
                vector.tensor_single_scalar(zhi[:].bitcast(U16), zhi[:].bitcast(U16),
                                            0x0F0F, ALU.bitwise_and)
                for q4 in range(4):
                    vector.wait_ge(s_zdma, 16 * (ci + 1))
                    for par, nib in ((0, zlo), (1, zhi)):
                        dst = _par_view2(zs_gu_s[rows, 2752 * q4:2752 * (q4 + 1)], par)
                        sv = _par_view2(snat_gu[:], par)
                        inst = vector.tensor_tensor(
                            dst, nib[:, 1376 * q4:1376 * (q4 + 1)], sv, ALU.mult)
                    vdum(vector).then_inc(s_zsb, 1)
                    ci += 1
            zq16d = zq_dn[:].bitcast(U16)
            vector.tensor_single_scalar(zdlo[:].bitcast(U16), zq16d, 0x0F0F, ALU.bitwise_and)
            vector.tensor_single_scalar(zdhi[:].bitcast(U16), zq16d, 4, ALU.logical_shift_right)
            vector.tensor_single_scalar(zdhi[:].bitcast(U16), zdhi[:].bitcast(U16),
                                        0x0F0F, ALU.bitwise_and)
            for q4 in range(4):
                vector.wait_ge(s_zdma, 16 * (8 + q4 + 1))
                for par, nib in ((0, zdlo), (1, zdhi)):
                    dst = _par_view2(zs_d_s[0:86, 1024 * q4:1024 * (q4 + 1)], par)
                    sv = _par_view2(snat_dn[:], par)
                    inst = vector.tensor_tensor(
                        dst, nib[:, 512 * q4:512 * (q4 + 1)], sv, ALU.mult)
                vdum(vector).then_inc(s_zsb, 1)
            vdum(vector).then_inc(s_zs, 1)

    es_zsb.close()

    es_B = ExitStack()
    pk_gu = [[sb(es_B, f"pk{m}{b}", [128, 2048], U8) for b in range(2)] for m in range(2)]
    ssl = [[[sb(es_B, f"ssl{m}{b}{p}", [128, 2048], BF16) for p in range(2)]
            for b in range(2)] for m in range(2)]
    wsl = [[sb(es_B, f"wsl{m}{b}", [128, 4096], BF16) for b in range(2)] for m in range(2)]
    lo_b = sb(es_B, "lob", [128, 2048], U8)
    hi_b = sb(es_B, "hib", [128, 2048], U8)
    silu_t = [sb(es_B, f"silu{b}", [128, 1024], BF16) for b in range(2)]
    h_t = [sb(es_B, f"htile{j}", [128, 1024], BF16) for j in range(3)]
    es_psB = ExitStack()
    ps_gu = es_psB.enter_context(nc.psum_tensor("psgu", [128, 4096], F32))

    qgu_d = [qg_d, qu_d]
    srep_d = [sg_rep_d, su_rep_d]
    BARRIER_A2 = BARRIER_A + [(s_zs, 1)]

    def _emit_hmult(vector, tp):
        bp_ = tp % 2
        vector.wait_ge(s_sl, tp + 1)
        vector.wait_ge(s_U, tp + 1)
        if tp >= 3:
            vector.wait_ge(shw_sl[tp % 3], 16 * (tp // 3))
        vdum(vector)
        uoff = (bp_ * 4 + 2) * 512
        vector.tensor_tensor(
            h_t[tp % 3][:], ps_gu[:, uoff:uoff + 1024], silu_t[bp_][:],
            ALU.mult)
        vdum(vector).then_inc(s_hm, 1)

    with nc.Block() as block:

        @block.sync
        def _(sync):
            for sem, vv in BARRIER_A2:
                sync.wait_ge(sem, vv)
            for t in range(NT):
                if t >= 2:
                    sync.wait_ge(s_wrdy, t - 1)
                for m in range(2):
                    sync.dma_start(out=pk_gu[m][t % 2][:], in_=qgu_d[m][t]).then_inc(spk_sl[t % 2], 16)

        @block.gpsimd
        def _(gp):
            for sem, vv in BARRIER_A2:
                gp.wait_ge(sem, vv)
            for t in range(NT):
                if t >= 2:
                    gp.wait_ge(s_wrdy, t - 1)
                for m in range(2):
                    for par in range(2):
                        gp.dma_start(out=ssl[m][t % 2][par][:],
                                     in_=srep_d[m][t, par]).then_inc(ssr_sl[t % 2], 16)

        @block.vector
        def _(vector):
            for t in range(NT):
                b = t % 2
                vector.wait_ge(spk_sl[t % 2], 32 * (t // 2 + 1))
                vector.wait_ge(ssr_sl[t % 2], 64 * (t // 2 + 1))
                if t >= 2:
                    vector.wait_ge(s_G, t - 1)
                    vector.wait_ge(s_U, t - 1)
                for m in range(2):
                    pk16 = pk_gu[m][b][:].bitcast(U16)
                    vector.tensor_single_scalar(lo_b[:].bitcast(U16), pk16, 0x0F0F, ALU.bitwise_and)
                    vector.tensor_single_scalar(hi_b[:].bitcast(U16), pk16, 4, ALU.logical_shift_right)
                    vector.tensor_single_scalar(hi_b[:].bitcast(U16), hi_b[:].bitcast(U16),
                                                0x0F0F, ALU.bitwise_and)
                    for par, nib in ((0, lo_b), (1, hi_b)):
                        w4 = wsl[m][b][:].rearrange("p (kt c t) -> p kt c t", c=64, t=2)
                        w3 = w4[:, :, :, par:par + 1].squeeze(3)
                        n3 = nib[:].rearrange("p (kt c) -> p kt c", c=64)
                        s3 = ssl[m][b][par][:].rearrange("p (kt c) -> p kt c", c=64)
                        vector.tensor_tensor(w3, n3, s3, ALU.mult)
                vdum(vector).then_inc(s_wrdy, 1)
                if t >= 1:
                    _emit_hmult(vector, t - 1)
            _emit_hmult(vector, NT - 1)

        @block.scalar
        def _(scalar):
            for t in range(NT):
                b = t % 2
                scalar.wait_ge(s_G, t + 1)
                if t >= 2:
                    scalar.wait_ge(s_hm, t - 1)
                goff = (b * 4) * 512
                scalar.activation(silu_t[b][:, 0:512], ps_gu[:, goff:goff + 512], AF.Silu)
                scalar.activation(silu_t[b][:, 512:1024], ps_gu[:, goff + 512:goff + 1024],
                                  AF.Silu)
                sdum(scalar).then_inc(s_sl, 1)
                if t >= 1:
                    scalar.wait_ge(s_hm, t)
                    scalar.dma_start(out=h_d[t - 1],
                                     in_=h_t[(t - 1) % 3][:]).then_inc(shw_sl[(t - 1) % 3], 16)
            scalar.wait_ge(s_hm, NT)
            scalar.dma_start(out=h_d[NT - 1],
                             in_=h_t[(NT - 1) % 3][:]).then_inc(shw_sl[(NT - 1) % 3], 16)

        @block.tensor
        def _(tensor):
            tensor.wait_ge(s_zs, 1)
            tensor.wait_ge(s_agse, 1)
            tensor.wait_ge(s_ngsd, 32)
            tensor.wait_ge(s_atpe, 16 * 8)
            for t in range(NT):
                b = t % 2
                tensor.wait_ge(s_wrdy, t + 1)
                if t >= 2:
                    tensor.wait_ge(s_sl, t - 1)
                    tensor.wait_ge(s_hm, t - 1)
                for m in range(2):
                    for mh in range(2):
                        off = (b * 4 + m * 2 + mh) * 512
                        for kt in range(KT):
                            tensor.matmul(
                                ps_gu[:, off:off + 512],
                                wsl[m][b][:, 128 * kt:128 * (kt + 1)],
                                xn2T_s[:, 1024 * kt + 512 * mh:1024 * kt + 512 * (mh + 1)],
                                start=(kt == 0), stop=False)
                        tensor.matmul(
                            ps_gu[:, off:off + 512],
                            zs_gu_s[32 * m:32 * m + 32, 128 * t:128 * (t + 1)],
                            ngs_hi_s[32 * m:32 * m + 32, 512 * mh:512 * (mh + 1)],
                            start=False, stop=False)
                        inst = tensor.matmul(
                            ps_gu[:, off:off + 512],
                            zs_gu_s[32 * m:32 * m + 32, 128 * t:128 * (t + 1)],
                            ngs_lo_s[32 * m:32 * m + 32, 512 * mh:512 * (mh + 1)],
                            start=False, stop=True)
                    inst.then_inc(s_G if m == 0 else s_U, 1)

    es_psB.close()
    es_B.close()
    es_xn2.close()

    es_C = ExitStack()
    gshf_s = sb(es_C, "gshf", [86, M], F32)
    ngsh_hi_s = sb(es_C, "ngshhi", [128, M], BF16)
    ngsh_lo_s = sb(es_C, "ngshlo", [128, M], BF16)
    oh86_s = sb(es_C, "oh86s", [128, 7396], BF16)
    hT = [sb(es_C, f"hTr{j}", [128, 1024], BF16) for j in range(4)]
    pkd = [sb(es_C, f"pkdn{j}", [128, 2048], U8) for j in range(2)]
    sds = [sb(es_C, f"sdsl{j}", [128, 2048], BF16) for j in range(2)]
    wd_sl = [sb(es_C, f"wdsl{j}", [128, 512], BF16) for j in range(2)]
    lod = sb(es_C, "lod", [128, 256], U8)
    hid = sb(es_C, "hid", [128, 256], U8)
    x1sl = [sb(es_C, f"x1sl{j}", [128, 512], F32) for j in range(8)]
    outst = [sb(es_C, f"outst{j}", [128, 512], F32) for j in range(8)]
    es_psG = ExitStack()
    ps_gsh = es_psG.enter_context(nc.psum_tensor("psgsh", [86, M], F32))

    BARRIER_B = [(s_U, NT), (s_hm, NT), (s_sl, NT),
                 (shw_sl[0], 16 * 29), (shw_sl[1], 16 * 29), (shw_sl[2], 16 * 28)]

    with nc.Block() as block:

        @block.sync
        def _(sync):
            for sem, vv in BARRIER_B:
                sync.wait_ge(sem, vv)
            sync.dma_start(out=oh86_s[:], in_=oh86_d[:]).then_inc(s_oh, 16)

        @block.scalar
        def _(scalar):
            for sem, vv in BARRIER_B:
                scalar.wait_ge(sem, vv)
            for i in range(IT):
                if i >= 4:
                    scalar.wait_ge(s_gti, i - 3)
                scalar.dma_start(out=hT[i % 4][:], in_=h_d[i]).then_inc(sghr_sl[i % 4], 16)

        @block.tensor
        def _(tensor):
            tensor.wait_ge(s_oh, 16)
            for i in range(IT):
                tensor.wait_ge(sghr_sl[i % 4], 16 * (i // 4 + 1))
                for mh in range(2):
                    inst = tensor.matmul(
                        ps_gsh[:, 512 * mh:512 * (mh + 1)],
                        oh86_s[:, 86 * i:86 * (i + 1)],
                        hT[i % 4][:, 512 * mh:512 * (mh + 1)],
                        start=(i == 0), stop=(i == IT - 1))
                inst.then_inc(s_gti, 1)

        @block.vector
        def _(vector):
            vector.memset(ngsh_hi_s[:], 0.0)
            vector.memset(ngsh_lo_s[:], 0.0)
            vector.wait_ge(s_gti, IT)
            vdum(vector)
            vector.tensor_scalar(gshf_s[:], ps_gsh[:], -1.0, None, ALU.mult)
            vector.tensor_copy(ngsh_hi_s[0:86, :], gshf_s[:])
            vector.tensor_tensor(ngsh_lo_s[0:86, :], gshf_s[:], ngsh_hi_s[0:86, :],
                                 ALU.subtract)
            vdum(vector).then_inc(s_gshe, 1)

    es_psG.close()

    es_psC = ExitStack()
    ps_dn = es_psC.enter_context(nc.psum_tensor("psdn", [128, 4096], F32))

    GSH_TOT = [22, 22, 21, 21]
    PKB = []
    for og in range(OG):
        for ib in range(11):
            i0 = 8 * ib
            cnt = min(8, IT - i0)
            if cnt > 0:
                PKB.append((og, i0, cnt))
    SDB = []
    for og in range(OG):
        for ib in range(22):
            i0 = 4 * ib
            cnt = min(4, IT - i0)
            if cnt > 0:
                SDB.append((og, i0, cnt))

    def pkb_of(og, i):
        return og * 11 + i // 8

    def sdb_of(og, i):
        return og * 22 + i // 4

    def consumed_pkb(g):
        og, i0, cnt = PKB[g]
        return og * IT + i0 + cnt

    def consumed_sdb(g):
        og, i0, cnt = SDB[g]
        return og * IT + i0 + cnt

    def _emit_cev(vector, og):
        for mt in range(MT):
            j = og * MT + mt
            vector.wait_ge(s_cmm, og + 1)
            vector.wait_ge(sx1r_sl[j % 8], 16 * (j // 8 + 1))
            if j >= 8:
                vector.wait_ge(sout_sl[j % 8], 16 * (j // 8))
            vdum(vector)
            vector.scalar_tensor_tensor(
                outst[j % 8][:], ps_dn[:, 512 * mt:512 * (mt + 1)], 1.0,
                x1sl[j % 8][:], ALU.mult, ALU.add)
            vdum(vector).then_inc(s_cev, 1)

    with nc.Block() as block:

        @block.sync
        def _(sync):
            sync.wait_ge(s_gshe, 1)
            for g, (og, i0, cnt) in enumerate(PKB):
                if g >= 2:
                    sync.wait_ge(s_wd, consumed_pkb(g - 2))
                dst = pkd[g % 2][:].rearrange("p (it c) -> p it c", c=256)[:, 0:cnt, :]
                sync.dma_start(out=dst,
                               in_=qd_d[og, i0:i0 + cnt].transpose([1, 0, 2])).then_inc(spkd_sl[g % 2], 16)

        @block.gpsimd
        def _(gp):
            gp.wait_ge(s_gshe, 1)

            def emit_sd(g):
                ogb, i0, cnt = SDB[g]
                if g >= 2:
                    gp.wait_ge(s_wd, consumed_sdb(g - 2))
                dst = sds[g % 2][:].rearrange("p (it q) -> p it q", q=512)[:, 0:cnt, :]
                srcp = sd_rep_d[ogb, i0:i0 + cnt].transpose([2, 0, 1, 3])
                gp.dma_start(out=dst, in_=srcp).then_inc(ssd_sl[g % 2], 16)

            def emit_x1read(j):
                if j >= 8:
                    gp.wait_ge(s_cev, j - 7)
                og_r, mt_r = j // MT, j % MT
                gp.dma_start(
                    out=x1sl[j % 8][:],
                    in_=x1_d[128 * mt_r:128 * (mt_r + 1), 512 * og_r:512 * (og_r + 1)]
                ).then_inc(sx1r_sl[j % 8], 16)

            def emit_outw(jj):
                gp.wait_ge(s_cev, jj + 1)
                og_w, mt_w = jj // MT, jj % MT
                gp.dma_start(
                    out=out_d[128 * mt_w:128 * (mt_w + 1), 512 * og_w:512 * (og_w + 1)],
                    in_=outst[jj % 8][:]).then_inc(sout_sl[jj % 8], 16)

            for g in range(22):
                emit_sd(g)
            for og in range(OG):
                if og + 1 < OG:
                    emit_sd(22 * (og + 1))
                    emit_sd(22 * (og + 1) + 1)
                for mt in range(MT):
                    emit_x1read(og * MT + mt)
                if og >= 1:
                    for mt in range(MT):
                        emit_outw((og - 1) * MT + mt)
                if og + 1 < OG:
                    for g in range(22 * (og + 1) + 2, 22 * (og + 2)):
                        emit_sd(g)
            for mt in range(MT):
                emit_outw((OG - 1) * MT + mt)
            for jslot in range(8):
                gp.wait_ge(sout_sl[jslot], 16 * 8)

        @block.scalar
        def _(scalar):
            scalar.wait_ge(s_gshe, 1)
            GSH_TOT = [22, 22, 21, 21]
            for og in range(OG):
                for i in range(IT):
                    idx = og * IT + i
                    if idx >= 4:
                        scalar.wait_ge(s_cti, idx - 3)
                    scalar.dma_start(out=hT[idx % 4][:],
                                     in_=h_d[i]).then_inc(sghr_sl[idx % 4], 16)

        @block.vector
        def _(vector):
            vector.wait_ge(s_gshe, 1)
            for og in range(OG):
                for i in range(IT):
                    idx = og * IT + i
                    if i == 2 and og >= 1:
                        _emit_cev(vector, og - 1)
                    vector.wait_ge(spkd_sl[pkb_of(og, i) % 2], 16 * (pkb_of(og, i) // 2 + 1))
                    vector.wait_ge(ssd_sl[sdb_of(og, i) % 2], 16 * (sdb_of(og, i) // 2 + 1))
                    if idx >= 2:
                        vector.wait_ge(s_cti, idx - 1)
                    pk = pkd[pkb_of(og, i) % 2][:, 256 * (i % 8):256 * (i % 8 + 1)]
                    pk16 = pk.bitcast(U16)
                    sdslot = sds[sdb_of(og, i) % 2][:, 512 * (i % 4):512 * (i % 4 + 1)]
                    vector.tensor_single_scalar(lod[:].bitcast(U16), pk16, 0x0F0F, ALU.bitwise_and)
                    vector.tensor_single_scalar(hid[:].bitcast(U16), pk16, 4, ALU.logical_shift_right)
                    vdum(vector)
                    vector.tensor_single_scalar(hid[:].bitcast(U16), hid[:].bitcast(U16),
                                                0x0F0F, ALU.bitwise_and)
                    vdum(vector)
                    vector.tensor_tensor(_par_view2(wd_sl[idx % 2][:], 0), lod[:],
                                         sdslot[:, 0:256], ALU.mult)
                    vdum(vector)
                    vector.tensor_tensor(_par_view2(wd_sl[idx % 2][:], 1), hid[:],
                                         sdslot[:, 256:512], ALU.mult)
                    vdum(vector)
                    vdum(vector).then_inc(s_wd, 1)
            _emit_cev(vector, OG - 1)

        @block.tensor
        def _(tensor):
            tensor.wait_ge(s_gshe, 1)
            for og in range(OG):
                if og >= 1:
                    tensor.wait_ge(s_cev, og * MT)
                for i in range(IT):
                    idx = og * IT + i
                    tensor.wait_ge(s_wd, idx + 1)
                    tensor.wait_ge(sghr_sl[idx % 4], 16 * (GSH_TOT[idx % 4] + idx // 4 + 1))
                    for mt in range(MT):
                        inst = tensor.matmul(
                            ps_dn[:, 512 * mt:512 * (mt + 1)],
                            hT[idx % 4][:, 128 * mt:128 * (mt + 1)],
                            wd_sl[idx % 2][:],
                            start=(i == 0), stop=False)
                    inst.then_inc(s_cti, 1)
                for mt in range(MT):
                    tensor.matmul(
                        ps_dn[:, 512 * mt:512 * (mt + 1)],
                        ngsh_hi_s[:, 128 * mt:128 * (mt + 1)],
                        zs_d_s[:, 512 * og:512 * (og + 1)],
                        start=False, stop=False)
                    inst = tensor.matmul(
                        ps_dn[:, 512 * mt:512 * (mt + 1)],
                        ngsh_lo_s[:, 128 * mt:128 * (mt + 1)],
                        zs_d_s[:, 512 * og:512 * (og + 1)],
                        start=False, stop=True)
                inst.then_inc(s_cmm, 1)

    es_psC.close()
    es_C.close()
    return nc


def kernel(**inputs):
    consts = _host_consts()
    x_full = np.ascontiguousarray(
        np.asarray(inputs["L0_attn_u"], dtype=np.float32).reshape(8192, DM))
    common = {
        "lam": np.ascontiguousarray(inputs["L0_attn_Lambda_ri"], dtype=np.float32),
        "pv": np.ascontiguousarray(inputs["L0_attn_p_ri"], dtype=np.float32),
        "qv": np.ascontiguousarray(inputs["L0_attn_q_ri"], dtype=np.float32),
        "vc": np.ascontiguousarray(
            np.asarray(inputs["L0_attn_Vc_ri"], dtype=np.float32).reshape(64, 128)),
        "ct": np.ascontiguousarray(
            np.asarray(inputs["L0_attn_Ct_ri"], dtype=np.float32).reshape(64, 2)),
        "bp": np.ascontiguousarray(inputs["L0_attn_Bp"], dtype=np.float32),
        "dd": np.asarray(inputs["L0_attn_D"], dtype=np.float32).reshape(1, 1),
        "lsr": np.ascontiguousarray(np.broadcast_to(
            np.asarray(inputs["L0_attn_log_step"], dtype=np.float32).reshape(1, 1), (64, 1))),
        "wre": consts["wre"], "wim": consts["wim"], "c6464": consts["c6464"],
        "ones164": consts["ones164"], "id128": consts["id128"],
        "oh32": consts["oh32"], "oh86": consts["oh86"],
        "qg": _tile_qweight_gu(np.ascontiguousarray(inputs["L0_ffn_gate_qweight"])),
        "qu": _tile_qweight_gu(np.ascontiguousarray(inputs["L0_ffn_up_qweight"])),
        "qd": _tile_qweight_down(np.ascontiguousarray(inputs["L0_ffn_down_qweight"])),
        "sg_rep": _srep_gu(np.asarray(inputs["L0_ffn_gate_scales"], dtype=np.float32)),
        "su_rep": _srep_gu(np.asarray(inputs["L0_ffn_up_scales"], dtype=np.float32)),
        "sd_rep": _srep_down(np.asarray(inputs["L0_ffn_down_scales"], dtype=np.float32)),
        "sg_nat": np.ascontiguousarray(inputs["L0_ffn_gate_scales"], dtype=np.float32),
        "su_nat": np.ascontiguousarray(inputs["L0_ffn_up_scales"], dtype=np.float32),
        "sd_nat": np.ascontiguousarray(inputs["L0_ffn_down_scales"], dtype=np.float32),
        "zg": np.ascontiguousarray(inputs["L0_ffn_gate_qzeros"]).view(np.uint8),
        "zu": np.ascontiguousarray(inputs["L0_ffn_up_qzeros"]).view(np.uint8),
        "zd": np.ascontiguousarray(inputs["L0_ffn_down_qzeros"]).view(np.uint8),
    }
    nc = build_nc()
    in_maps = []
    for c in range(NCORES):
        mm_ = dict(common)
        mm_["x"] = np.ascontiguousarray(x_full[M * c:M * (c + 1)])
        in_maps.append(mm_)
    res = run_bass_kernel_spmd(nc, in_maps, core_ids=list(range(NCORES)))
    out = np.concatenate([r["out"] for r in res.results], axis=0)
    return out.reshape(2, 4096, DM)




# version 13 (final)
# speedup vs baseline: 1.0028x; 1.0028x over previous
import numpy as np
import ml_dtypes
from contextlib import ExitStack
import concourse.bass as bass
import concourse.mybir as mybir
from concourse.bass_utils import run_bass_kernel_spmd

F32 = mybir.dt.float32
BF16 = mybir.dt.bfloat16
U8 = mybir.dt.uint8
U16 = mybir.dt.uint16
AF = mybir.ActivationFunctionType
ALU = mybir.AluOpType
AX = mybir.AxisListType
bfdt = ml_dtypes.bfloat16

NCORES = 8
M = 1024
MT = 8
DM = 4096
KT = 32
NT = 86
IDIM = 11008
OG = 8
IT = 86
L = 4096
INV_DM = 1.0 / 4096.0
EPS = 1e-5


def _host_consts():
    ar = np.arange(L, dtype=np.float32) / np.float32(L)
    Omega = np.exp((-2j * np.pi * ar).astype(np.complex64)).astype(np.complex64)
    w = ((np.complex64(1.0) - Omega) / (np.complex64(1.0) + Omega)).astype(np.complex64)
    c = (np.complex64(2.0) / (np.complex64(1.0) + Omega)).astype(np.complex64)
    cre = c.real.reshape(64, 64).astype(np.float32)
    cim = c.imag.reshape(64, 64).astype(np.float32)
    F64 = np.exp(2j * np.pi * np.arange(64)[:, None] * np.arange(64)[None, :] / 64.0) / 64.0
    tw = np.exp(2j * np.pi * np.arange(64)[:, None] * np.arange(64)[None, :] / 4096.0)
    c8 = np.stack([
        cre, cim,
        F64.real.astype(np.float32), F64.imag.astype(np.float32),
        (-F64.imag).astype(np.float32),
        tw.real.astype(np.float32), tw.imag.astype(np.float32),
        np.eye(64, dtype=np.float32),
    ]).astype(np.float32)
    oh32 = np.zeros((128, 32 * 32), dtype=np.float32)
    for I in range(32):
        oh32[:, 32 * I + I] = 1.0
    oh86 = np.zeros((128, 86 * 86), dtype=np.float32)
    for I in range(86):
        oh86[:, 86 * I + I] = 1.0
    return {
        "wre": np.ascontiguousarray(w.real.reshape(1, L).astype(np.float32)),
        "wim": np.ascontiguousarray(w.imag.reshape(1, L).astype(np.float32)),
        "c6464": c8,
        "ones164": np.ones((1, 64), dtype=np.float32),
        "id128": np.eye(128, dtype=np.float32).astype(bfdt),
        "oh32": oh32.astype(bfdt),
        "oh86": oh86.astype(bfdt),
    }


def _tile_qweight_gu(qw):
    q8 = qw.view(np.uint8)
    t = q8.reshape(KT, 128, NT, 64).transpose(2, 1, 0, 3)
    return np.ascontiguousarray(t.reshape(NT, 128, 2048))


def _tile_qweight_down(qw):
    q8 = qw.view(np.uint8)
    t = q8.reshape(IT, 128, OG, 256).transpose(2, 0, 1, 3)
    return np.ascontiguousarray(t)


def _srep_gu(sc):
    s4 = sc.reshape(KT, NT, 64, 2)
    t = s4.transpose(1, 3, 0, 2).reshape(NT, 2, 2048)
    out = np.broadcast_to(t[:, :, None, :], (NT, 2, 128, 2048))
    return np.ascontiguousarray(out.astype(np.float32))


def _srep_down(sc):
    s4 = sc.reshape(IT, OG, 256, 2)
    t = s4.transpose(1, 0, 3, 2)
    out = np.broadcast_to(t[:, :, :, None, :], (OG, IT, 2, 128, 256))
    return np.ascontiguousarray(out.astype(np.float32))


def _par_view2(ap2d, par, period=2):
    n = ap2d.shape[1]
    r = ap2d.rearrange("p (c t) -> p c t", t=period)
    return r[:, :, par:par + 1].squeeze(2)


def build_nc():
    nc = bass.Bass("TRN2", target_bir_lowering=False)
    dt = nc.dram_tensor
    x_d = dt("x", [M, DM], F32, kind="ExternalInput").ap()
    lam_d = dt("lam", [64, 2], F32, kind="ExternalInput").ap()
    pv_d = dt("pv", [64, 2], F32, kind="ExternalInput").ap()
    qv_d = dt("qv", [64, 2], F32, kind="ExternalInput").ap()
    vc_d = dt("vc", [64, 128], F32, kind="ExternalInput").ap()
    ct_d = dt("ct", [64, 2], F32, kind="ExternalInput").ap()
    bp_d = dt("bp", [64, 1], F32, kind="ExternalInput").ap()
    dd_d = dt("dd", [1, 1], F32, kind="ExternalInput").ap()
    lsr_d = dt("lsr", [64, 1], F32, kind="ExternalInput").ap()
    wre_d = dt("wre", [1, L], F32, kind="ExternalInput").ap()
    wim_d = dt("wim", [1, L], F32, kind="ExternalInput").ap()
    c64_d = dt("c6464", [8, 64, 64], F32, kind="ExternalInput").ap()
    ones164_d = dt("ones164", [1, 64], F32, kind="ExternalInput").ap()
    id128_d = dt("id128", [128, 128], BF16, kind="ExternalInput").ap()
    oh32_d = dt("oh32", [128, 1024], BF16, kind="ExternalInput").ap()
    oh86_d = dt("oh86", [128, 7396], BF16, kind="ExternalInput").ap()
    qg_d = dt("qg", [NT, 128, 2048], U8, kind="ExternalInput").ap()
    qu_d = dt("qu", [NT, 128, 2048], U8, kind="ExternalInput").ap()
    qd_d = dt("qd", [OG, IT, 128, 256], U8, kind="ExternalInput").ap()
    sg_rep_d = dt("sg_rep", [NT, 2, 128, 2048], F32, kind="ExternalInput").ap()
    su_rep_d = dt("su_rep", [NT, 2, 128, 2048], F32, kind="ExternalInput").ap()
    sd_rep_d = dt("sd_rep", [OG, IT, 2, 128, 256], F32, kind="ExternalInput").ap()
    sg_nat_d = dt("sg_nat", [32, IDIM], F32, kind="ExternalInput").ap()
    su_nat_d = dt("su_nat", [32, IDIM], F32, kind="ExternalInput").ap()
    sd_nat_d = dt("sd_nat", [IT, DM], F32, kind="ExternalInput").ap()
    zg_d = dt("zg", [32, 5504], U8, kind="ExternalInput").ap()
    zu_d = dt("zu", [32, 5504], U8, kind="ExternalInput").ap()
    zd_d = dt("zd", [IT, 2048], U8, kind="ExternalInput").ap()
    out_d = dt("out", [M, DM], F32, kind="ExternalOutput").ap()
    x1_d = dt("x1buf", [M, DM], F32, kind="Internal").ap()
    h_d = dt("hbuf", [IT, 128, M], BF16, kind="Internal").ap()
    kb_d = dt("kbuf", [1, 8320], BF16, kind="Internal").ap()
    kv_d = dt("kvbuf", [8, 4096], F32, kind="Internal").ap()

    es_glob = ExitStack()

    def sb(es, name, shape, dtp, **kw):
        return es.enter_context(nc.sbuf_tensor(name, shape, dtp, **kw))

    id128_s = sb(es_glob, "id128s", [128, 128], BF16)
    lam_s = sb(es_glob, "lams", [64, 2], F32)
    pv_s = sb(es_glob, "pvs", [64, 2], F32)
    qv_s = sb(es_glob, "qvs", [64, 2], F32)
    vc_s = sb(es_glob, "vcs", [64, 128], F32)
    ct_s = sb(es_glob, "cts", [64, 2], F32)
    bp_s = sb(es_glob, "bps", [64, 1], F32)
    dd_s = sb(es_glob, "dds", [1, 1], F32)
    lsr_s = sb(es_glob, "lsrs", [64, 1], F32)
    c64_s = sb(es_glob, "c64s", [64, 512], F32)
    ones164_s = sb(es_glob, "ones164s", [1, 64], F32)
    ts2_s = sb(es_glob, "ts2s", [64, 2], F32)
    oh32_s = sb(es_glob, "oh32s", [128, 1024], BF16)
    sum_s = sb(es_glob, "sums", [128, 2], F32)
    ssq_s = sb(es_glob, "ssqs", [128, 2], F32)
    mean_s = sb(es_glob, "means", [128, 2], F32)
    msq_s = sb(es_glob, "msqs", [128, 2], F32)
    var_s = sb(es_glob, "vars", [128, 2], F32)
    sdc_s = sb(es_glob, "sdcols", [128, 2], F32)
    rstd_s = sb(es_glob, "rstds", [128, 2], F32)
    gsf_s = sb(es_glob, "gsf", [32, M], F32)
    ngs_hi_s = sb(es_glob, "ngshi", [64, M], BF16)
    ngs_lo_s = sb(es_glob, "ngslo", [64, M], BF16)
    vdj_s = sb(es_glob, "vdjunk", [128, 192], F32)
    eps_s = sb(es_glob, "epscol", [128, 1], F32)
    sdj_s = sb(es_glob, "sdjunk", [128, 64], F32)
    zs_gu_s = sb(es_glob, "zsgu", [64, IDIM], BF16)
    zs_d_s = sb(es_glob, "zsd", [128, DM], BF16)

    es_xn2 = ExitStack()
    xn2T_s = sb(es_xn2, "xn2T", [128, KT * M], BF16, side="right")

    def mk(name):
        return es_glob.enter_context(nc.semaphore(name))
    s_small = mk("ssmall"); s_x = mk("sx"); s_x1w = mk("sx1w")
    s_k = mk("sk"); s_strip = mk("sstrip")
    s_asq = mk("sasq"); s_ava = mk("sava"); s_asd = mk("sasd")
    s_axn = mk("saxn"); s_atp = mk("satp"); s_atpe = mk("satpe")
    s_acv = mk("sacv"); s_ace = mk("sace"); s_ags = mk("sags")
    s_agse = mk("sagse"); s_zs = mk("szs"); s_zdma = mk("szdma"); s_zsb = mk("szsb")
    s_pk = mk("spk"); s_sr = mk("ssr"); s_wrdy = mk("swrdy")
    s_G = mk("sG"); s_U = mk("sU"); s_sl = mk("ssl")
    s_hm = mk("shm"); s_hw = mk("shw")
    s_oh = mk("soh"); s_ghr = mk("sghr"); s_gti = mk("sgti"); s_gshe = mk("sgshe")
    s_pkd = mk("spkd"); s_sd = mk("ssd"); s_wd = mk("swd")
    s_cti = mk("scti"); s_cmm = mk("scmm"); s_x1r = mk("sx1r")
    s_cev = mk("scev"); s_out = mk("sout"); s_hr2 = mk("shr2")
    sx_sl = [mk("sxa"), mk("sxb")]
    sx1w_sl = [mk("sx1wa"), mk("sx1wb")]
    spk_sl = [mk("spka"), mk("spkb")]
    ssr_sl = [mk("ssra"), mk("ssrb")]
    shw_sl = [mk("shwa"), mk("shwb"), mk("shwc")]
    sghr_sl = [mk(f"sghr{j}") for j in range(4)]
    spkd_sl = [mk("spkda"), mk("spkdb")]
    ssd_sl = [mk("ssda"), mk("ssdb")]
    sx1r_sl = [mk(f"sx1r{j}") for j in range(8)]
    sout_sl = [mk(f"sout{j}") for j in range(8)]
    s_zq = mk("szq"); s_ngsd = mk("sngsd")
    s_kd = mk("skd"); s_kg = mk("skg")

    def vdum(v):
        return v.tensor_copy(vdj_s[:], vdj_s[:])

    def sdum(s):
        return s.activation(sdj_s[:], oh32_s[:, 0:64], AF.Identity, bias=0.0, scale=1.0)

    es_A = ExitStack()
    xsl = [sb(es_A, f"xsl{j}", [128, DM], F32) for j in range(2)]
    xnsl = [sb(es_A, f"xnsl{j}", [128, DM], BF16) for j in range(2)]
    xn1T = [sb(es_A, f"xn1T{j}", [128, DM], BF16) for j in range(2)]
    strip_s = sb(es_A, "strip", [128, 8192], BF16)
    wtab = [sb(es_A, f"wtab{j}", [1, 256], F32) for j in range(2)]
    den_re_s = sb(es_A, "denre", [64, 256], F32)
    den_im_s = sb(es_A, "denim", [64, 256], F32)
    mag_s = sb(es_A, "mags", [64, 256], F32)
    mag2_s = sb(es_A, "mags2", [64, 256], F32)
    ef_s = sb(es_A, "efs", [128, 256], F32)
    kvch_s = sb(es_A, "kvch", [8, 256], F32)
    atile_s = sb(es_A, "atile", [64, 8 * 64], F32)
    scr_s = sb(es_A, "scr64", [64, 12 * 64], F32)
    vcreT_s = sb(es_A, "vcret", [64, 64], F32)
    vcimT_s = sb(es_A, "vcimt", [64, 64], F32)
    bre_s = sb(es_A, "bres", [64, 1], F32)
    bim_s = sb(es_A, "bims", [64, 1], F32)
    vmat_s = sb(es_A, "vmats", [128, 8], F32)
    krows_s = sb(es_A, "krows", [64, 64], BF16)
    zerot_s = sb(es_A, "zerot", [128, 33], BF16)

    es_psA = ExitStack()
    ps_tp = [es_psA.enter_context(nc.psum_tensor(f"pstp{j}", [128, 512], F32)) for j in range(2)]
    ps_tpb = [t[:].bitcast(BF16) for t in ps_tp]
    ps_gs = es_psA.enter_context(nc.psum_tensor("psgs", [32, M], F32))
    ps_quad = es_psA.enter_context(nc.psum_tensor("psquad", [128, 2048], F32))
    ps_wre = ps_quad[0:64, 0:256]
    ps_wim = ps_quad[0:64, 512:768]
    ps_kv = ps_quad[0:8, 1024:1280]
    ps_c1 = ps_quad[0:64, 1536:1600]
    ps_c2 = ps_quad[0:64, 1600:1664]
    ps_vt = ps_quad[0:64, 1664:1728]
    ps_b = ps_quad[0:64, 1728:1729]

    ident64 = c64_s[:, 7 * 64:8 * 64]
    lam_re = lam_s[:, 0:1]; lam_im = lam_s[:, 1:2]
    ts2col = ts2_s[:, 1:2]

    def scr(i):
        return scr_s[:, 64 * i:64 * (i + 1)]

    def at(i):
        return atile_s[:, 64 * i:64 * (i + 1)]

    ksched = []

    def kstep(eng_key, inc, fn):
        ksched.append((eng_key, inc, fn))

    kstep("vector", 1, lambda v: v.memset(zerot_s[:], 0.0))
    kstep("scalar", 1, lambda s: s.activation(ts2_s[:, 0:1], lsr_s[:], AF.Exp))

    def _v_ts2(v):
        v.reciprocal(ts2_s[:, 1:2], ts2_s[:, 0:1])
        vdum(v)
        return v.tensor_scalar_mul(ts2col, ts2col, 2.0)
    kstep("vector", 1, _v_ts2)
    def _t_tpre(t):
        t.transpose(ps_vt, vc_s[:, 0:128:2], ident64)
        return t.matmul(ps_vt[:, 63:64], vc_s[:, 0:128:2], ident64[:, 63:64],
                        start=True, stop=True)
    kstep("tensor", 1, _t_tpre)
    kstep("vector", 1, lambda v: v.tensor_copy(vcreT_s[:], ps_vt))
    def _t_tpim(t):
        t.transpose(ps_vt, vc_s[:, 1:128:2], ident64)
        return t.matmul(ps_vt[:, 63:64], vc_s[:, 1:128:2], ident64[:, 63:64],
                        start=True, stop=True)
    kstep("tensor", 1, _t_tpim)
    kstep("vector", 1, lambda v: v.tensor_copy(vcimT_s[:], ps_vt))
    kstep("tensor", 1, lambda t: t.matmul(ps_b, vcreT_s[:], bp_s[:], start=True, stop=True))
    kstep("vector", 1, lambda v: v.tensor_copy(bre_s[:], ps_b))
    kstep("tensor", 1, lambda t: t.matmul(ps_b, vcimT_s[:], bp_s[:], start=True, stop=True))
    kstep("vector", 1, lambda v: v.tensor_copy(bim_s[:], ps_b))

    def _v_quartet(v):
        pairs = [
            (ct_s[:, 0:1], ct_s[:, 1:2], bre_s[:], bim_s[:]),
            (ct_s[:, 0:1], ct_s[:, 1:2], pv_s[:, 0:1], pv_s[:, 1:2]),
            (qv_s[:, 0:1], qv_s[:, 1:2], bre_s[:], bim_s[:]),
            (qv_s[:, 0:1], qv_s[:, 1:2], pv_s[:, 0:1], pv_s[:, 1:2]),
        ]
        for idx, (are, aim, brr, bii) in enumerate(pairs):
            c = scr_s[:, 4 * idx:4 * idx + 4]
            v.tensor_mul(c[:, 0:1], are, brr)
            v.tensor_mul(c[:, 1:2], aim, bii)
            v.tensor_mul(c[:, 2:3], are, bii)
            v.tensor_mul(c[:, 3:4], aim, brr)
        vdum(v)
        for idx in range(4):
            c = scr_s[:, 4 * idx:4 * idx + 4]
            v.tensor_add(scr_s[:, 16 + 2 * idx:17 + 2 * idx], c[:, 0:1], c[:, 1:2])
            v.tensor_sub(scr_s[:, 17 + 2 * idx:18 + 2 * idx], c[:, 2:3], c[:, 3:4])
        vdum(v)
        last = None
        for idx in range(4):
            vre = scr_s[:, 16 + 2 * idx:17 + 2 * idx]
            vim = scr_s[:, 17 + 2 * idx:18 + 2 * idx]
            cre_c, cim_c = 2 * idx, 2 * idx + 1
            v.tensor_copy(vmat_s[0:64, cre_c:cre_c + 1], vre)
            v.tensor_copy(vmat_s[0:64, cim_c:cim_c + 1], vim)
            v.tensor_copy(vmat_s[64:128, cre_c:cre_c + 1], vim)
            last = v.tensor_scalar_mul(vmat_s[64:128, cim_c:cim_c + 1], vre, -1.0)
        return last
    kstep("vector", 1, _v_quartet)

    for c8 in range(16):
        sl = 0
        kstep("sync", 16, lambda sy, c8=c8, sl=sl: sy.dma_start(
            out=wtab[sl][:], in_=wre_d[0:1, 256 * c8:256 * (c8 + 1)]))
        kstep("sync", 16, lambda sy, c8=c8, sl=sl: sy.dma_start(
            out=wtab[sl + 1][:], in_=wim_d[0:1, 256 * c8:256 * (c8 + 1)]))

        def _t_bc(t, sl=sl):
            t.matmul(ps_wre, ones164_s[:], wtab[sl][:], start=True, stop=True)
            return t.matmul(ps_wim, ones164_s[:], wtab[sl + 1][:], start=True, stop=True)
        kstep("tensor", 1, _t_bc)

        def _v_den(v):
            v.tensor_scalar(den_re_s[:], ps_wre, ts2col, lam_re, ALU.mult, ALU.subtract)
            v.tensor_scalar(den_im_s[:], ps_wim, ts2col, lam_im, ALU.mult, ALU.subtract)
            v.tensor_mul(mag_s[:], den_re_s[:], den_re_s[:])
            v.tensor_mul(mag2_s[:], den_im_s[:], den_im_s[:])
            v.tensor_add(mag_s[:], mag_s[:], mag2_s[:])
            v.reciprocal(mag_s[:], mag_s[:])
            v.tensor_mul(ef_s[0:64, :], den_re_s[:], mag_s[:])
            return v.tensor_mul(ef_s[64:128, :], den_im_s[:], mag_s[:])
        kstep("vector", 1, _v_den)
        kstep("tensor", 1, lambda t: t.matmul(ps_kv, vmat_s[:], ef_s[:], start=True, stop=True))
        kstep("vector", 1, lambda v: v.tensor_copy(kvch_s[:], ps_kv))
        kstep("gpsimd", 16, lambda g, c8=c8: g.dma_start(
            out=kv_d[:, 256 * c8:256 * (c8 + 1)], in_=kvch_s[:]))

    def _g_kvback(g):
        for v in range(8):
            g.dma_start(
                out=at(v),
                in_=kv_d[v:v + 1, :].rearrange("a (p f) -> (a p) f", f=64)
            ).then_inc(s_kg, 16)
        return None
    kstep("gpsimd", 16 * 8, _g_kvback)

    def _v_rational(v):
        tre = scr(0); tim = at(7)
        v.tensor_scalar(tre, at(6), 1.0, None, ALU.add)
        mg = scr(1); m2 = scr(2)
        v.tensor_mul(mg, tre, tre)
        v.tensor_mul(m2, tim, tim)
        v.tensor_add(mg, mg, m2)
        v.reciprocal(mg, mg)
        nre = scr(3); nim = scr(4)
        v.tensor_mul(m2, at(2), at(4))
        v.tensor_mul(nre, at(3), at(5))
        v.tensor_sub(nre, m2, nre)
        v.tensor_mul(m2, at(2), at(5))
        v.tensor_mul(nim, at(3), at(4))
        v.tensor_add(nim, m2, nim)
        wr = scr(5); wi = scr(6)
        v.tensor_mul(m2, nre, tre)
        v.tensor_mul(wr, nim, tim)
        v.tensor_add(wr, m2, wr)
        v.tensor_mul(wr, wr, mg)
        v.tensor_mul(m2, nim, tre)
        v.tensor_mul(wi, nre, tim)
        v.tensor_sub(wi, m2, wi)
        v.tensor_mul(wi, wi, mg)
        are = scr(7); aim = scr(8)
        v.tensor_sub(are, at(0), wr)
        v.tensor_sub(aim, at(1), wi)
        cre_t = c64_s[:, 0:64]; cim_t = c64_s[:, 64:128]
        atre = scr(9); atim = scr(10)
        v.tensor_mul(m2, are, cre_t)
        v.tensor_mul(atre, aim, cim_t)
        v.tensor_sub(atre, m2, atre)
        v.tensor_mul(m2, are, cim_t)
        v.tensor_mul(atim, aim, cre_t)
        return v.tensor_add(atim, m2, atim)
    kstep("vector", 1, _v_rational)

    f64re = c64_s[:, 2 * 64:3 * 64]
    f64im = c64_s[:, 3 * 64:4 * 64]
    f64imn = c64_s[:, 4 * 64:5 * 64]
    twre = c64_s[:, 5 * 64:6 * 64]
    twim = c64_s[:, 6 * 64:7 * 64]

    def _t_sstep(t):
        t.matmul(ps_c1, scr(9), f64re, start=True, stop=False)
        t.matmul(ps_c1, scr(10), f64imn, start=False, stop=True)
        t.matmul(ps_c2, scr(9), f64im, start=True, stop=False)
        return t.matmul(ps_c2, scr(10), f64re, start=False, stop=True)
    kstep("tensor", 1, _t_sstep)

    def _v_twiddle(v):
        sre = scr(0); sim = scr(1)
        v.tensor_copy(sre, ps_c1)
        v.tensor_copy(sim, ps_c2)
        m2 = scr(2); tr = scr(3); ti = scr(4)
        v.tensor_mul(m2, sre, twre)
        v.tensor_mul(tr, sim, twim)
        v.tensor_sub(tr, m2, tr)
        v.tensor_mul(m2, sre, twim)
        v.tensor_mul(ti, sim, twre)
        return v.tensor_add(ti, m2, ti)
    kstep("vector", 1, _v_twiddle)

    def _t_xstep(t):
        t.matmul(ps_c1, scr(3), f64re, start=True, stop=False)
        return t.matmul(ps_c1, scr(4), f64imn, start=False, stop=True)
    kstep("tensor", 1, _t_xstep)

    def _v_xev(v):
        v.tensor_copy(scr(5), ps_c1)
        return v.tensor_tensor(scr(5)[0:1, 0:1], scr(5)[0:1, 0:1], dd_s[:], ALU.add)
    kstep("vector", 1, _v_xev)
    def _t_tpk(t):
        t.transpose(ps_c2, scr(5), ident64)
        return t.matmul(ps_c2[:, 63:64], scr(5), ident64[:, 63:64],
                        start=True, stop=True)
    kstep("tensor", 1, _t_tpk)
    kstep("vector", 1, lambda v: v.tensor_copy(krows_s[:], ps_c2))

    def _g_kbuf(g):
        g.dma_start(out=kb_d[0:1, 0:4224], in_=zerot_s[:]).then_inc(s_kg, 16)
        g.dma_start(out=kb_d[0:1, 4224:8320], in_=krows_s[:]).then_inc(s_kg, 16)
        return None
    kstep("gpsimd", 32, _g_kbuf)

    def _sc_strip(s):
        for p in range(128):
            s.dma_start(out=strip_s[p:p + 1, :],
                        in_=kb_d[0:1, 128 - p:128 - p + 8192]).then_inc(s_strip, 16)
        return None
    kstep("scalar", 0, _sc_strip)

    def run_ksched(key, eng):
        te = 0
        td = 0
        tg = 0
        for ek, inc, fn in ksched:
            if ek == key:
                if te > 0:
                    eng.wait_ge(s_k, te)
                if td > 0:
                    eng.wait_ge(s_kd, td)
                if tg > 0:
                    eng.wait_ge(s_kg, tg)
                if ek == "vector" and (te + td + tg) > 0:
                    vdum(eng)
                last = fn(eng)
                if inc > 0 and last is not None:
                    if ek == "vector":
                        vdum(eng).then_inc(s_k, inc)
                    elif ek == "scalar":
                        sdum(eng).then_inc(s_k, inc)
                    elif ek == "sync":
                        last.then_inc(s_kd, inc)
                    elif ek == "gpsimd":
                        last.then_inc(s_kg, inc)
                    else:
                        last.then_inc(s_k, inc)
            if ek == "sync":
                td += inc
            elif ek == "gpsimd":
                tg += inc
            else:
                te += inc

    def conv_ops(half):
        ops = []
        for I in range(32):
            for g4 in range(4):
                nbase = half * 2048 + g4 * 512
                Imax = min((nbase + 511) // 128, 31)
                if I <= Imax:
                    ops.append((I, g4, nbase, I == 0, I == Imax))
        return ops
    CONV = [conv_ops(0), conv_ops(1)]

    small_list = [
        (lam_s, lam_d), (pv_s, pv_d), (qv_s, qv_d), (vc_s, vc_d),
        (ct_s, ct_d), (bp_s, bp_d), (dd_s, dd_d), (lsr_s, lsr_d),
        (id128_s, id128_d), (ones164_s, ones164_d), (oh32_s, oh32_d),
    ]
    SMALL_TOT = 16 * (len(small_list) + 1)

    xn2T3 = xn2T_s[:].rearrange("p (I m) -> p I m", m=M)

    with nc.Block() as block:

        @block.sync
        def _(sync):
            for dst, src in small_list:
                sync.dma_start(out=dst[:], in_=src[:]).then_inc(s_small, 16)
            sync.dma_start(
                out=c64_s[:].rearrange("p (t f) -> p t f", f=64),
                in_=c64_d[:].transpose([1, 0, 2]),
            ).then_inc(s_small, 16)
            run_ksched("sync", sync)

        @block.gpsimd
        def _(gp):
            for i in range(2):
                gp.dma_start(out=xsl[i % 2][:], in_=x_d[128 * i:128 * (i + 1), :]).then_inc(sx_sl[i % 2], 16)
            gp.wait_ge(s_small, SMALL_TOT)
            run_ksched("gpsimd", gp)
            for i in range(2, MT):
                gp.wait_ge(sx1w_sl[i % 2], 16 * ((i - 2) // 2 + 1))
                gp.wait_ge(s_axn, 2 * i - 2)
                gp.dma_start(out=xsl[i % 2][:], in_=x_d[128 * i:128 * (i + 1), :]).then_inc(sx_sl[i % 2], 16)

        @block.scalar
        def _(scalar):
            scalar.wait_ge(s_small, SMALL_TOT)
            run_ksched("scalar", scalar)
            for i in range(MT):
                scalar.wait_ge(s_ava, 2 * i + 1)
                scalar.activation(sdc_s[:, 0:1], var_s[:, 0:1], AF.Sqrt, bias=eps_s[:])
                sdum(scalar).then_inc(s_asd, 1)
                scalar.wait_ge(s_ava, 2 * i + 2)
                scalar.activation(sdc_s[:, 1:2], var_s[:, 1:2], AF.Sqrt, bias=eps_s[:])
                sdum(scalar).then_inc(s_asd, 1)
                scalar.dma_start(out=x1_d[128 * i:128 * (i + 1), :],
                                 in_=xsl[i % 2][:]).then_inc(sx1w_sl[i % 2], 16)

        @block.tensor
        def _(tensor):
            tensor.wait_ge(s_small, SMALL_TOT)
            run_ksched("tensor", tensor)
            for i in range(MT):
                tensor.wait_ge(s_axn, 2 * i + 1)
                for g in range(8):
                    gidx = 16 * i + g
                    if gidx >= 2:
                        tensor.wait_ge(s_atpe, gidx - 1)
                    for j in range(4):
                        col = 4 * g + j
                        tensor.transpose(
                            ps_tpb[g % 2][:, 128 * j:128 * (j + 1)],
                            xnsl[i % 2][:, 128 * col:128 * (col + 1)], id128_s[:])
                    tensor.matmul(
                        ps_tp[g % 2][:, 511:512],
                        xnsl[i % 2][:, 128 * (4 * g + 3):128 * (4 * g + 4)],
                        id128_s[:, 127:128], start=True, stop=True).then_inc(s_atp, 1)
                if i == 0:
                    tensor.wait_ge(s_strip, 16 * 128)
                tensor.wait_ge(s_atpe, 16 * i + 8)
                if i > 0:
                    tensor.wait_ge(s_ace, 2 * i)
                for half in range(2):
                    if half == 1:
                        tensor.wait_ge(s_ace, 2 * i + 1)
                    for (I, g4, nbase, first, lastf) in CONV[half]:
                        f0 = nbase - 128 * I + 4096
                        inst = tensor.matmul(
                            ps_quad[:, 512 * g4:512 * (g4 + 1)],
                            xn1T[i % 2][:, 128 * I:128 * (I + 1)],
                            strip_s[:, f0:f0 + 512],
                            start=first, stop=lastf)
                    inst.then_inc(s_acv, 1)
                tensor.wait_ge(s_axn, 2 * i + 2)
                for g in range(8):
                    gidx = 16 * i + 8 + g
                    tensor.wait_ge(s_atpe, gidx - 1)
                    for j in range(4):
                        col = 4 * g + j
                        tensor.transpose(
                            ps_tpb[g % 2][:, 128 * j:128 * (j + 1)],
                            xnsl[i % 2][:, 128 * col:128 * (col + 1)], id128_s[:])
                    tensor.matmul(
                        ps_tp[g % 2][:, 511:512],
                        xnsl[i % 2][:, 128 * (4 * g + 3):128 * (4 * g + 4)],
                        id128_s[:, 127:128], start=True, stop=True).then_inc(s_atp, 1)
                tensor.wait_ge(s_atpe, 16 * (i + 1))
                for I in range(KT):
                    inst = tensor.matmul(
                        ps_gs[:, 128 * i:128 * (i + 1)],
                        oh32_s[:, 32 * I:32 * (I + 1)],
                        xn2T3[:, I, 128 * i:128 * (i + 1)],
                        start=(I == 0), stop=(I == KT - 1))
                if i == MT - 1:
                    inst.then_inc(s_ags, 1)

        @block.vector
        def _(vector):
            vector.memset(eps_s[:], EPS)
            vector.memset(vdj_s[:], 0.0)
            vector.wait_ge(s_small, SMALL_TOT)
            run_ksched("vector", vector)
            for i in range(MT):
                b = i % 2
                vector.wait_ge(sx_sl[i % 2], 16 * (i // 2 + 1))
                if i >= 2:
                    vector.wait_ge(s_atp, 16 * (i - 1))
                vector.reduce_sum(sum_s[:, 0:1], xsl[b][:], AX.X)
                vector.tensor_mul(xnsl[b][:], xsl[b][:], xsl[b][:])
                vector.reduce_sum(ssq_s[:, 0:1], xnsl[b][:], AX.X)
                vector.tensor_scalar(mean_s[:, 0:1], sum_s[:, 0:1], INV_DM, None, ALU.mult)
                vdum(vector)
                vector.tensor_mul(var_s[:, 0:1], mean_s[:, 0:1], mean_s[:, 0:1])
                vdum(vector)
                vector.scalar_tensor_tensor(
                    var_s[:, 0:1], ssq_s[:, 0:1], INV_DM, var_s[:, 0:1],
                    ALU.mult, ALU.subtract)
                vdum(vector).then_inc(s_ava, 1)
                vector.wait_ge(s_asd, 2 * i + 1)
                vdum(vector)
                vector.reciprocal(rstd_s[:, 0:1], sdc_s[:, 0:1])
                vdum(vector)
                vector.tensor_scalar(
                    xnsl[b][:], xsl[b][:], mean_s[:, 0:1], rstd_s[:, 0:1],
                    ALU.subtract, ALU.mult)
                vdum(vector).then_inc(s_axn, 1)
                for g in range(8):
                    gidx = 16 * i + g
                    vector.wait_ge(s_atp, gidx + 1)
                    vdum(vector)
                    vector.tensor_copy(
                        xn1T[b][:, 512 * g:512 * (g + 1)], ps_tpb[g % 2][:, 0:512])
                    vdum(vector).then_inc(s_atpe, 1)
                for half in range(2):
                    vector.wait_ge(s_acv, 2 * i + half + 1)
                    vdum(vector)
                    vector.scalar_tensor_tensor(
                        xsl[b][:, 2048 * half:2048 * (half + 1)],
                        ps_quad[:, 0:2048], 1.0,
                        xsl[b][:, 2048 * half:2048 * (half + 1)],
                        ALU.mult, ALU.add).then_inc(s_ace, 1)
                vector.reduce_sum(sum_s[:, 1:2], xsl[b][:], AX.X)
                vector.tensor_mul(xnsl[b][:], xsl[b][:], xsl[b][:])
                vector.reduce_sum(ssq_s[:, 1:2], xnsl[b][:], AX.X)
                vector.tensor_scalar(mean_s[:, 1:2], sum_s[:, 1:2], INV_DM, None, ALU.mult)
                vdum(vector)
                vector.tensor_mul(var_s[:, 1:2], mean_s[:, 1:2], mean_s[:, 1:2])
                vdum(vector)
                vector.scalar_tensor_tensor(
                    var_s[:, 1:2], ssq_s[:, 1:2], INV_DM, var_s[:, 1:2],
                    ALU.mult, ALU.subtract)
                vdum(vector).then_inc(s_ava, 1)
                vector.wait_ge(s_asd, 2 * i + 2)
                vdum(vector)
                vector.reciprocal(rstd_s[:, 1:2], sdc_s[:, 1:2])
                vdum(vector)
                vector.tensor_scalar(
                    xnsl[b][:], xsl[b][:], mean_s[:, 1:2], rstd_s[:, 1:2],
                    ALU.subtract, ALU.mult)
                vdum(vector).then_inc(s_axn, 1)
                for g in range(8):
                    gidx = 16 * i + 8 + g
                    vector.wait_ge(s_atp, gidx + 1)
                    vdum(vector)
                    dst2 = xn2T3[:, 4 * g:4 * g + 4, 128 * i:128 * (i + 1)]
                    src2 = ps_tpb[g % 2][:, 0:512].rearrange("p (j r) -> p j r", r=128)
                    vector.tensor_copy(dst2, src2)
                    vdum(vector).then_inc(s_atpe, 1)
            vector.wait_ge(s_ags, 1)
            vdum(vector)
            vector.tensor_scalar(gsf_s[:], ps_gs[:], -1.0, None, ALU.mult)
            vector.tensor_copy(ngs_hi_s[0:32, :], gsf_s[:])
            vector.tensor_tensor(ngs_lo_s[0:32, :], gsf_s[:], ngs_hi_s[0:32, :],
                                 ALU.subtract)
            vdum(vector).then_inc(s_agse, 1)

    es_psA.close()
    es_A.close()

    es_zsb = ExitStack()
    zq_g2 = sb(es_zsb, "zqg2", [32, 5504], U8)
    zq_u2 = sb(es_zsb, "zqu2", [32, 5504], U8)
    zq_dn = sb(es_zsb, "zqdn", [86, 2048], U8)
    zlo = sb(es_zsb, "zlo", [32, 5504], U8)
    zhi = sb(es_zsb, "zhi", [32, 5504], U8)
    zdlo = sb(es_zsb, "zdlo", [86, 2048], U8)
    zdhi = sb(es_zsb, "zdhi", [86, 2048], U8)
    snat_gu = sb(es_zsb, "snatgu", [32, 2752], F32)
    snat_dn = sb(es_zsb, "snatd", [86, 1024], F32)

    BARRIER_A = [(s_ags, 1), (s_agse, 1), (s_asd, 16),
                 (sx1w_sl[0], 16 * 4), (sx1w_sl[1], 16 * 4),
                 (sx_sl[0], 16 * 4), (sx_sl[1], 16 * 4)]

    with nc.Block() as block:

        @block.sync
        def _(sync):
            for sem, vv in BARRIER_A:
                sync.wait_ge(sem, vv)
            sync.dma_start(out=ngs_hi_s[32:64, :], in_=ngs_hi_s[0:32, :]).then_inc(s_ngsd, 16)
            sync.dma_start(out=ngs_lo_s[32:64, :], in_=ngs_lo_s[0:32, :]).then_inc(s_ngsd, 16)
            sync.dma_start(out=zq_g2[:], in_=zg_d[:]).then_inc(s_zq, 16)
            sync.dma_start(out=zq_u2[:], in_=zu_d[:]).then_inc(s_zq, 16)
            sync.dma_start(out=zq_dn[:], in_=zd_d[:]).then_inc(s_zq, 16)
            for mi, srcd in enumerate([sg_nat_d, su_nat_d]):
                for q4 in range(4):
                    ci = mi * 4 + q4
                    if ci > 0:
                        sync.wait_ge(s_zsb, ci)
                    sync.dma_start(out=snat_gu[:],
                                   in_=srcd[:, 2752 * q4:2752 * (q4 + 1)]).then_inc(s_zdma, 16)
            for q4 in range(4):
                ci = 8 + q4
                sync.wait_ge(s_zsb, ci)
                sync.dma_start(out=snat_dn[:],
                               in_=sd_nat_d[:, 1024 * q4:1024 * (q4 + 1)]).then_inc(s_zdma, 16)

        @block.vector
        def _(vector):
            for sem, vv in BARRIER_A:
                vector.wait_ge(sem, vv)
            vector.memset(zs_d_s[:], 0.0)
            vector.wait_ge(s_zq, 48)
            ci = 0
            for mi, zq_m in enumerate([zq_g2, zq_u2]):
                rows = slice(32 * mi, 32 * mi + 32)
                zq16 = zq_m[:].bitcast(U16)
                vector.tensor_single_scalar(zlo[:].bitcast(U16), zq16, 0x0F0F, ALU.bitwise_and)
                vector.tensor_single_scalar(zhi[:].bitcast(U16), zq16, 4, ALU.logical_shift_right)
                vector.tensor_single_scalar(zhi[:].bitcast(U16), zhi[:].bitcast(U16),
                                            0x0F0F, ALU.bitwise_and)
                for q4 in range(4):
                    vector.wait_ge(s_zdma, 16 * (ci + 1))
                    for par, nib in ((0, zlo), (1, zhi)):
                        dst = _par_view2(zs_gu_s[rows, 2752 * q4:2752 * (q4 + 1)], par)
                        sv = _par_view2(snat_gu[:], par)
                        inst = vector.tensor_tensor(
                            dst, nib[:, 1376 * q4:1376 * (q4 + 1)], sv, ALU.mult)
                    vdum(vector).then_inc(s_zsb, 1)
                    ci += 1
            zq16d = zq_dn[:].bitcast(U16)
            vector.tensor_single_scalar(zdlo[:].bitcast(U16), zq16d, 0x0F0F, ALU.bitwise_and)
            vector.tensor_single_scalar(zdhi[:].bitcast(U16), zq16d, 4, ALU.logical_shift_right)
            vector.tensor_single_scalar(zdhi[:].bitcast(U16), zdhi[:].bitcast(U16),
                                        0x0F0F, ALU.bitwise_and)
            for q4 in range(4):
                vector.wait_ge(s_zdma, 16 * (8 + q4 + 1))
                for par, nib in ((0, zdlo), (1, zdhi)):
                    dst = _par_view2(zs_d_s[0:86, 1024 * q4:1024 * (q4 + 1)], par)
                    sv = _par_view2(snat_dn[:], par)
                    inst = vector.tensor_tensor(
                        dst, nib[:, 512 * q4:512 * (q4 + 1)], sv, ALU.mult)
                vdum(vector).then_inc(s_zsb, 1)
            vdum(vector).then_inc(s_zs, 1)

    es_zsb.close()

    es_B = ExitStack()
    pk_gu = [[sb(es_B, f"pk{m}{b}", [128, 2048], U8) for b in range(2)] for m in range(2)]
    ssl = [[[sb(es_B, f"ssl{m}{b}{p}", [128, 2048], BF16) for p in range(2)]
            for b in range(2)] for m in range(2)]
    wsl = [[sb(es_B, f"wsl{m}{b}", [128, 4096], BF16) for b in range(2)] for m in range(2)]
    lo_b = sb(es_B, "lob", [128, 2048], U8)
    hi_b = sb(es_B, "hib", [128, 2048], U8)
    silu_t = [sb(es_B, f"silu{b}", [128, 1024], BF16) for b in range(2)]
    h_t = [sb(es_B, f"htile{j}", [128, 1024], BF16) for j in range(3)]
    es_psB = ExitStack()
    ps_gu = es_psB.enter_context(nc.psum_tensor("psgu", [128, 4096], F32))

    qgu_d = [qg_d, qu_d]
    srep_d = [sg_rep_d, su_rep_d]
    BARRIER_A2 = BARRIER_A + [(s_zs, 1)]

    def _emit_hmult(vector, tp):
        bp_ = tp % 2
        vector.wait_ge(s_sl, tp + 1)
        vector.wait_ge(s_U, tp + 1)
        if tp >= 3:
            vector.wait_ge(shw_sl[tp % 3], 16 * (tp // 3))
        vdum(vector)
        uoff = (bp_ * 4 + 2) * 512
        vector.tensor_tensor(
            h_t[tp % 3][:], ps_gu[:, uoff:uoff + 1024], silu_t[bp_][:],
            ALU.mult)
        vdum(vector).then_inc(s_hm, 1)

    with nc.Block() as block:

        @block.sync
        def _(sync):
            for sem, vv in BARRIER_A2:
                sync.wait_ge(sem, vv)
            for t in range(NT):
                if t >= 2:
                    sync.wait_ge(s_wrdy, t - 1)
                for m in range(2):
                    sync.dma_start(out=pk_gu[m][t % 2][:], in_=qgu_d[m][t]).then_inc(spk_sl[t % 2], 16)

        @block.gpsimd
        def _(gp):
            for sem, vv in BARRIER_A2:
                gp.wait_ge(sem, vv)
            for t in range(NT):
                if t >= 2:
                    gp.wait_ge(s_wrdy, t - 1)
                for m in range(2):
                    for par in range(2):
                        gp.dma_start(out=ssl[m][t % 2][par][:],
                                     in_=srep_d[m][t, par]).then_inc(ssr_sl[t % 2], 16)

        @block.vector
        def _(vector):
            for t in range(NT):
                b = t % 2
                vector.wait_ge(spk_sl[t % 2], 32 * (t // 2 + 1))
                vector.wait_ge(ssr_sl[t % 2], 64 * (t // 2 + 1))
                if t >= 2:
                    vector.wait_ge(s_G, t - 1)
                    vector.wait_ge(s_U, t - 1)
                for m in range(2):
                    pk16 = pk_gu[m][b][:].bitcast(U16)
                    vector.tensor_single_scalar(lo_b[:].bitcast(U16), pk16, 0x0F0F, ALU.bitwise_and)
                    vector.tensor_single_scalar(hi_b[:].bitcast(U16), pk16, 4, ALU.logical_shift_right)
                    vector.tensor_single_scalar(hi_b[:].bitcast(U16), hi_b[:].bitcast(U16),
                                                0x0F0F, ALU.bitwise_and)
                    for par, nib in ((0, lo_b), (1, hi_b)):
                        w4 = wsl[m][b][:].rearrange("p (kt c t) -> p kt c t", c=64, t=2)
                        w3 = w4[:, :, :, par:par + 1].squeeze(3)
                        n3 = nib[:].rearrange("p (kt c) -> p kt c", c=64)
                        s3 = ssl[m][b][par][:].rearrange("p (kt c) -> p kt c", c=64)
                        vector.tensor_tensor(w3, n3, s3, ALU.mult)
                vdum(vector).then_inc(s_wrdy, 1)
                if t >= 1:
                    _emit_hmult(vector, t - 1)
            _emit_hmult(vector, NT - 1)

        @block.scalar
        def _(scalar):
            for t in range(NT):
                b = t % 2
                scalar.wait_ge(s_G, t + 1)
                if t >= 2:
                    scalar.wait_ge(s_hm, t - 1)
                goff = (b * 4) * 512
                scalar.activation(silu_t[b][:, 0:512], ps_gu[:, goff:goff + 512], AF.Silu)
                scalar.activation(silu_t[b][:, 512:1024], ps_gu[:, goff + 512:goff + 1024],
                                  AF.Silu)
                sdum(scalar).then_inc(s_sl, 1)
                if t >= 1:
                    scalar.wait_ge(s_hm, t)
                    scalar.dma_start(out=h_d[t - 1],
                                     in_=h_t[(t - 1) % 3][:]).then_inc(shw_sl[(t - 1) % 3], 16)
            scalar.wait_ge(s_hm, NT)
            scalar.dma_start(out=h_d[NT - 1],
                             in_=h_t[(NT - 1) % 3][:]).then_inc(shw_sl[(NT - 1) % 3], 16)

        @block.tensor
        def _(tensor):
            tensor.wait_ge(s_zs, 1)
            tensor.wait_ge(s_agse, 1)
            tensor.wait_ge(s_ngsd, 32)
            tensor.wait_ge(s_atpe, 16 * 8)
            for t in range(NT):
                b = t % 2
                tensor.wait_ge(s_wrdy, t + 1)
                if t >= 2:
                    tensor.wait_ge(s_sl, t - 1)
                    tensor.wait_ge(s_hm, t - 1)
                for m in range(2):
                    for mh in range(2):
                        off = (b * 4 + m * 2 + mh) * 512
                        for kt in range(KT):
                            tensor.matmul(
                                ps_gu[:, off:off + 512],
                                wsl[m][b][:, 128 * kt:128 * (kt + 1)],
                                xn2T_s[:, 1024 * kt + 512 * mh:1024 * kt + 512 * (mh + 1)],
                                start=(kt == 0), stop=False)
                        tensor.matmul(
                            ps_gu[:, off:off + 512],
                            zs_gu_s[32 * m:32 * m + 32, 128 * t:128 * (t + 1)],
                            ngs_hi_s[32 * m:32 * m + 32, 512 * mh:512 * (mh + 1)],
                            start=False, stop=False)
                        inst = tensor.matmul(
                            ps_gu[:, off:off + 512],
                            zs_gu_s[32 * m:32 * m + 32, 128 * t:128 * (t + 1)],
                            ngs_lo_s[32 * m:32 * m + 32, 512 * mh:512 * (mh + 1)],
                            start=False, stop=True)
                    inst.then_inc(s_G if m == 0 else s_U, 1)

    es_psB.close()
    es_B.close()
    es_xn2.close()

    es_C = ExitStack()
    gshf_s = sb(es_C, "gshf", [86, M], F32)
    ngsh_hi_s = sb(es_C, "ngshhi", [128, M], BF16)
    ngsh_lo_s = sb(es_C, "ngshlo", [128, M], BF16)
    oh86_s = sb(es_C, "oh86s", [128, 7396], BF16)
    hT = [sb(es_C, f"hTr{j}", [128, 1024], BF16) for j in range(4)]
    pkd = [sb(es_C, f"pkdn{j}", [128, 2048], U8) for j in range(2)]
    sds = [sb(es_C, f"sdsl{j}", [128, 2048], BF16) for j in range(2)]
    wd_sl = [sb(es_C, f"wdsl{j}", [128, 512], BF16) for j in range(2)]
    lod = sb(es_C, "lod", [128, 256], U8)
    hid = sb(es_C, "hid", [128, 256], U8)
    x1sl = [sb(es_C, f"x1sl{j}", [128, 512], F32) for j in range(8)]
    outst = [sb(es_C, f"outst{j}", [128, 512], F32) for j in range(8)]
    es_psG = ExitStack()
    ps_gsh = es_psG.enter_context(nc.psum_tensor("psgsh", [86, M], F32))

    BARRIER_B = [(s_U, NT), (s_hm, NT), (s_sl, NT),
                 (shw_sl[0], 16 * 29), (shw_sl[1], 16 * 29), (shw_sl[2], 16 * 28)]

    with nc.Block() as block:

        @block.sync
        def _(sync):
            for sem, vv in BARRIER_B:
                sync.wait_ge(sem, vv)
            sync.dma_start(out=oh86_s[:], in_=oh86_d[:]).then_inc(s_oh, 16)

        @block.scalar
        def _(scalar):
            for sem, vv in BARRIER_B:
                scalar.wait_ge(sem, vv)
            for i in range(IT):
                if i >= 4:
                    scalar.wait_ge(s_gti, i - 3)
                scalar.dma_start(out=hT[i % 4][:], in_=h_d[i]).then_inc(sghr_sl[i % 4], 16)

        @block.tensor
        def _(tensor):
            tensor.wait_ge(s_oh, 16)
            for i in range(IT):
                tensor.wait_ge(sghr_sl[i % 4], 16 * (i // 4 + 1))
                for mh in range(2):
                    inst = tensor.matmul(
                        ps_gsh[:, 512 * mh:512 * (mh + 1)],
                        oh86_s[:, 86 * i:86 * (i + 1)],
                        hT[i % 4][:, 512 * mh:512 * (mh + 1)],
                        start=(i == 0), stop=(i == IT - 1))
                inst.then_inc(s_gti, 1)

        @block.vector
        def _(vector):
            vector.memset(ngsh_hi_s[:], 0.0)
            vector.memset(ngsh_lo_s[:], 0.0)
            vector.wait_ge(s_gti, IT)
            vdum(vector)
            vector.tensor_scalar(gshf_s[:], ps_gsh[:], -1.0, None, ALU.mult)
            vector.tensor_copy(ngsh_hi_s[0:86, :], gshf_s[:])
            vector.tensor_tensor(ngsh_lo_s[0:86, :], gshf_s[:], ngsh_hi_s[0:86, :],
                                 ALU.subtract)
            vdum(vector).then_inc(s_gshe, 1)

    es_psG.close()

    es_psC = ExitStack()
    ps_dn = es_psC.enter_context(nc.psum_tensor("psdn", [128, 4096], F32))

    GSH_TOT = [22, 22, 21, 21]
    PKB = []
    for og in range(OG):
        for ib in range(11):
            i0 = 8 * ib
            cnt = min(8, IT - i0)
            if cnt > 0:
                PKB.append((og, i0, cnt))
    SDB = []
    for og in range(OG):
        for ib in range(22):
            i0 = 4 * ib
            cnt = min(4, IT - i0)
            if cnt > 0:
                SDB.append((og, i0, cnt))

    def pkb_of(og, i):
        return og * 11 + i // 8

    def sdb_of(og, i):
        return og * 22 + i // 4

    def consumed_pkb(g):
        og, i0, cnt = PKB[g]
        return og * IT + i0 + cnt

    def consumed_sdb(g):
        og, i0, cnt = SDB[g]
        return og * IT + i0 + cnt

    def _emit_cev(vector, og):
        for mt in range(MT):
            j = og * MT + mt
            vector.wait_ge(s_cmm, og + 1)
            vector.wait_ge(sx1r_sl[j % 8], 16 * (j // 8 + 1))
            if j >= 8:
                vector.wait_ge(sout_sl[j % 8], 16 * (j // 8))
            vdum(vector)
            vector.scalar_tensor_tensor(
                outst[j % 8][:], ps_dn[:, 512 * mt:512 * (mt + 1)], 1.0,
                x1sl[j % 8][:], ALU.mult, ALU.add)
            vdum(vector).then_inc(s_cev, 1)

    with nc.Block() as block:

        @block.sync
        def _(sync):
            sync.wait_ge(s_gshe, 1)
            for g, (og, i0, cnt) in enumerate(PKB):
                if g >= 2:
                    sync.wait_ge(s_wd, consumed_pkb(g - 2))
                dst = pkd[g % 2][:].rearrange("p (it c) -> p it c", c=256)[:, 0:cnt, :]
                sync.dma_start(out=dst,
                               in_=qd_d[og, i0:i0 + cnt].transpose([1, 0, 2])).then_inc(spkd_sl[g % 2], 16)

        @block.gpsimd
        def _(gp):
            gp.wait_ge(s_gshe, 1)

            def emit_sd(g):
                ogb, i0, cnt = SDB[g]
                if g >= 2:
                    gp.wait_ge(s_wd, consumed_sdb(g - 2))
                dst = sds[g % 2][:].rearrange("p (it q) -> p it q", q=512)[:, 0:cnt, :]
                srcp = sd_rep_d[ogb, i0:i0 + cnt].transpose([2, 0, 1, 3])
                gp.dma_start(out=dst, in_=srcp).then_inc(ssd_sl[g % 2], 16)

            def emit_x1read(j):
                if j >= 8:
                    gp.wait_ge(s_cev, j - 7)
                og_r, mt_r = j // MT, j % MT
                gp.dma_start(
                    out=x1sl[j % 8][:],
                    in_=x1_d[128 * mt_r:128 * (mt_r + 1), 512 * og_r:512 * (og_r + 1)]
                ).then_inc(sx1r_sl[j % 8], 16)

            def emit_outw(jj):
                gp.wait_ge(s_cev, jj + 1)
                og_w, mt_w = jj // MT, jj % MT
                gp.dma_start(
                    out=out_d[128 * mt_w:128 * (mt_w + 1), 512 * og_w:512 * (og_w + 1)],
                    in_=outst[jj % 8][:]).then_inc(sout_sl[jj % 8], 16)

            for g in range(22):
                emit_sd(g)
            for og in range(OG):
                if og + 1 < OG:
                    emit_sd(22 * (og + 1))
                    emit_sd(22 * (og + 1) + 1)
                for mt in range(MT):
                    emit_x1read(og * MT + mt)
                if og >= 1:
                    for mt in range(MT):
                        emit_outw((og - 1) * MT + mt)
                if og + 1 < OG:
                    for g in range(22 * (og + 1) + 2, 22 * (og + 2)):
                        emit_sd(g)
            for mt in range(MT):
                emit_outw((OG - 1) * MT + mt)
            for jslot in range(8):
                gp.wait_ge(sout_sl[jslot], 16 * 8)

        @block.scalar
        def _(scalar):
            scalar.wait_ge(s_gshe, 1)
            GSH_TOT = [22, 22, 21, 21]
            for og in range(OG):
                for i in range(IT):
                    idx = og * IT + i
                    if idx >= 4:
                        scalar.wait_ge(s_cti, idx - 3)
                    scalar.dma_start(out=hT[idx % 4][:],
                                     in_=h_d[i]).then_inc(sghr_sl[idx % 4], 16)

        @block.vector
        def _(vector):
            vector.wait_ge(s_gshe, 1)
            for og in range(OG):
                for i in range(IT):
                    idx = og * IT + i
                    if i == 2 and og >= 1:
                        _emit_cev(vector, og - 1)
                    vector.wait_ge(spkd_sl[pkb_of(og, i) % 2], 16 * (pkb_of(og, i) // 2 + 1))
                    vector.wait_ge(ssd_sl[sdb_of(og, i) % 2], 16 * (sdb_of(og, i) // 2 + 1))
                    if idx >= 2:
                        vector.wait_ge(s_cti, idx - 1)
                    pk = pkd[pkb_of(og, i) % 2][:, 256 * (i % 8):256 * (i % 8 + 1)]
                    pk16 = pk.bitcast(U16)
                    sdslot = sds[sdb_of(og, i) % 2][:, 512 * (i % 4):512 * (i % 4 + 1)]
                    vector.tensor_single_scalar(lod[:].bitcast(U16), pk16, 0x0F0F, ALU.bitwise_and)
                    vector.tensor_single_scalar(hid[:].bitcast(U16), pk16, 4, ALU.logical_shift_right)
                    vdum(vector)
                    vector.tensor_single_scalar(hid[:].bitcast(U16), hid[:].bitcast(U16),
                                                0x0F0F, ALU.bitwise_and)
                    vdum(vector)
                    vector.tensor_tensor(_par_view2(wd_sl[idx % 2][:], 0), lod[:],
                                         sdslot[:, 0:256], ALU.mult)
                    vector.tensor_tensor(_par_view2(wd_sl[idx % 2][:], 1), hid[:],
                                         sdslot[:, 256:512], ALU.mult)
                    vdum(vector).then_inc(s_wd, 1)
            _emit_cev(vector, OG - 1)

        @block.tensor
        def _(tensor):
            tensor.wait_ge(s_gshe, 1)
            for og in range(OG):
                if og >= 1:
                    tensor.wait_ge(s_cev, og * MT)
                for i in range(IT):
                    idx = og * IT + i
                    tensor.wait_ge(s_wd, idx + 1)
                    tensor.wait_ge(sghr_sl[idx % 4], 16 * (GSH_TOT[idx % 4] + idx // 4 + 1))
                    for mt in range(MT):
                        inst = tensor.matmul(
                            ps_dn[:, 512 * mt:512 * (mt + 1)],
                            hT[idx % 4][:, 128 * mt:128 * (mt + 1)],
                            wd_sl[idx % 2][:],
                            start=(i == 0), stop=False)
                    inst.then_inc(s_cti, 1)
                for mt in range(MT):
                    tensor.matmul(
                        ps_dn[:, 512 * mt:512 * (mt + 1)],
                        ngsh_hi_s[:, 128 * mt:128 * (mt + 1)],
                        zs_d_s[:, 512 * og:512 * (og + 1)],
                        start=False, stop=False)
                    inst = tensor.matmul(
                        ps_dn[:, 512 * mt:512 * (mt + 1)],
                        ngsh_lo_s[:, 128 * mt:128 * (mt + 1)],
                        zs_d_s[:, 512 * og:512 * (og + 1)],
                        start=False, stop=True)
                inst.then_inc(s_cmm, 1)

    es_psC.close()
    es_C.close()
    return nc


def kernel(**inputs):
    consts = _host_consts()
    x_full = np.ascontiguousarray(
        np.asarray(inputs["L0_attn_u"], dtype=np.float32).reshape(8192, DM))
    common = {
        "lam": np.ascontiguousarray(inputs["L0_attn_Lambda_ri"], dtype=np.float32),
        "pv": np.ascontiguousarray(inputs["L0_attn_p_ri"], dtype=np.float32),
        "qv": np.ascontiguousarray(inputs["L0_attn_q_ri"], dtype=np.float32),
        "vc": np.ascontiguousarray(
            np.asarray(inputs["L0_attn_Vc_ri"], dtype=np.float32).reshape(64, 128)),
        "ct": np.ascontiguousarray(
            np.asarray(inputs["L0_attn_Ct_ri"], dtype=np.float32).reshape(64, 2)),
        "bp": np.ascontiguousarray(inputs["L0_attn_Bp"], dtype=np.float32),
        "dd": np.asarray(inputs["L0_attn_D"], dtype=np.float32).reshape(1, 1),
        "lsr": np.ascontiguousarray(np.broadcast_to(
            np.asarray(inputs["L0_attn_log_step"], dtype=np.float32).reshape(1, 1), (64, 1))),
        "wre": consts["wre"], "wim": consts["wim"], "c6464": consts["c6464"],
        "ones164": consts["ones164"], "id128": consts["id128"],
        "oh32": consts["oh32"], "oh86": consts["oh86"],
        "qg": _tile_qweight_gu(np.ascontiguousarray(inputs["L0_ffn_gate_qweight"])),
        "qu": _tile_qweight_gu(np.ascontiguousarray(inputs["L0_ffn_up_qweight"])),
        "qd": _tile_qweight_down(np.ascontiguousarray(inputs["L0_ffn_down_qweight"])),
        "sg_rep": _srep_gu(np.asarray(inputs["L0_ffn_gate_scales"], dtype=np.float32)),
        "su_rep": _srep_gu(np.asarray(inputs["L0_ffn_up_scales"], dtype=np.float32)),
        "sd_rep": _srep_down(np.asarray(inputs["L0_ffn_down_scales"], dtype=np.float32)),
        "sg_nat": np.ascontiguousarray(inputs["L0_ffn_gate_scales"], dtype=np.float32),
        "su_nat": np.ascontiguousarray(inputs["L0_ffn_up_scales"], dtype=np.float32),
        "sd_nat": np.ascontiguousarray(inputs["L0_ffn_down_scales"], dtype=np.float32),
        "zg": np.ascontiguousarray(inputs["L0_ffn_gate_qzeros"]).view(np.uint8),
        "zu": np.ascontiguousarray(inputs["L0_ffn_up_qzeros"]).view(np.uint8),
        "zd": np.ascontiguousarray(inputs["L0_ffn_down_qzeros"]).view(np.uint8),
    }
    nc = build_nc()
    in_maps = []
    for c in range(NCORES):
        mm_ = dict(common)
        mm_["x"] = np.ascontiguousarray(x_full[M * c:M * (c + 1)])
        in_maps.append(mm_)
    res = run_bass_kernel_spmd(nc, in_maps, core_ids=list(range(NCORES)))
    out = np.concatenate([r["out"] for r in res.results], axis=0)
    return out.reshape(2, 4096, DM)


